# version 05
# baseline (speedup 1.0000x reference)
import math
from contextlib import ExitStack

import numpy as np
import ml_dtypes
import concourse.bass as bass
import concourse.mybir as mybir
from concourse.bass_core_rust import IndirectOffsetOnAxis
from concourse.bass_utils import run_bass_kernel_spmd

F32 = mybir.dt.float32
BF16 = mybir.dt.bfloat16
I32 = mybir.dt.int32
AF = mybir.ActivationFunctionType
ALU = mybir.AluOpType
AX = mybir.AxisListType

B, T, E, H, D = 4, 4096, 2048, 16, 128
NE, HF, CAP = 8, 2816, 2048
LCAP = 512
NC = 8
NT = 2048
BF = ml_dtypes.bfloat16
INVSQ = 1.0 / math.sqrt(128.0)



class Em:
    ENGINES = ("sync", "scalar", "vector", "tensor", "gpsimd")

    def __init__(self, nc):
        self.nc = nc
        self.q = {e: [] for e in self.ENGINES}
        self.cnt = {}
        self.sems = {}
        self.hiwater = {e: {} for e in self.ENGINES}
        self._bregs = {}
        self.watermark = {}
        self.floor = []

    def sem(self, stack, *names):
        for n in names:
            self.sems[n] = stack.enter_context(self.nc.semaphore(n))
            self.cnt[n] = 0

    DMA_SEMS = ("sq", "aq", "gq", "wq")

    def phase(self):
        self.floor = self.phase_toks()

    def _waits(self, eng_name, waits):
        out = []
        hw = self.hiwater[eng_name]
        for s, v in list(waits) + self.floor:
            if s in self.DMA_SEMS:
                v = self.cnt[s]
            if v > 0 and hw.get(s, 0) < v:
                hw[s] = v
                out.append((s, v))
            if v > self.watermark.get(s, 0):
                self.watermark[s] = v
        return out

    def op(self, eng_name, method, *args, waits=(), sem=None, inc=0, **kwargs):
        waits = list(waits)
        if inc and sem in self.DMA_SEMS:
            wm = self.watermark.get(sem, 0)
            if wm > 0:
                waits.append((sem, wm))
        wl = self._waits(eng_name, waits)
        tok = None
        if inc:
            self.cnt[sem] += inc
            tok = (sem, self.cnt[sem])
        sems = self.sems

        bregs = self._bregs

        def play(eng, wl=wl, method=method, args=args, kwargs=kwargs, inc=inc, sem=sem):
            for s, v in wl:
                eng.wait_ge(sems[s], v)
            if (method == "indirect_dma_start"
                    and isinstance(kwargs.get("bounds_check"), int)):
                val = kwargs["bounds_check"]
                reg = bregs.get(val)
                if reg is None:
                    reg = eng.alloc_register(f"bnd{val}")
                    eng.reg_mov(reg, val)
                    bregs[val] = reg
                kwargs = dict(kwargs, bounds_check=reg)
            try:
                inst = getattr(eng, method)(*args, **kwargs)
            except Exception as e:
                raise RuntimeError(f"emit failed: {method} arg1={args[1] if len(args)>1 else None} arg2={args[2] if len(args)>2 else None} kw={kwargs}") from e
            if inc:
                inst.then_inc(sems[sem], inc)
        self.q[eng_name].append(play)
        return tok

    def t(self, method, *a, waits=(), last=True, **k):
        return self.op("tensor", method, *a, waits=waits,
                       sem="pe" if last else None, inc=1 if last else 0, **k)

    def v(self, method, *a, waits=(), **k):
        w = list(waits) + [self.tok("dve")]
        return self.op("vector", method, *a, waits=w, sem="dve", inc=1, **k)

    def s(self, method, *a, waits=(), **k):
        w = list(waits) + [self.tok("act")]
        return self.op("scalar", method, *a, waits=w, sem="act", inc=1, **k)

    def g(self, method, *a, waits=(), **k):
        w = list(waits) + [self.tok("pool")]
        return self.op("gpsimd", method, *a, waits=w, sem="pool", inc=1, **k)

    def dma(self, queue, out, in_, waits=(), sem=None, **k):
        sem = sem or {"sync": "sq", "scalar": "aq", "gpsimd": "gq"}[queue]
        return self.op(queue, "dma_start", waits=waits, sem=sem, inc=16,
                       out=out, in_=in_, **k)

    def dmat(self, queue, out, in_, waits=()):
        return self.dma(queue, out, in_, waits=waits, transpose=True)

    def gather(self, out, in_, idx_ap, bounds, waits=()):
        return self.op("gpsimd", "indirect_dma_start",
                       out, None, in_, IndirectOffsetOnAxis(idx_ap, 0),
                       waits=waits, sem="gq", inc=16,
                       bounds_check=bounds, oob_is_err=False)

    def scatter(self, out, idx_ap, in_, bounds, waits=()):
        return self.op("gpsimd", "indirect_dma_start",
                       out, IndirectOffsetOnAxis(idx_ap, 0), in_, None,
                       waits=waits, sem="gq", inc=16,
                       bounds_check=bounds, oob_is_err=False)

    def raw(self, eng_name, fn):
        self.q[eng_name].append(fn)

    def tok(self, sem):
        return (sem, self.cnt[sem])

    def phase_toks(self):
        return [(s, c) for s, c in self.cnt.items() if c > 0 and s != "pq"]

    def finish(self, eng_name="sync"):
        waits = [(s, c) for s, c in self.cnt.items() if c > 0]
        sems = self.sems

        def play(eng, waits=waits):
            for s, v in waits:
                eng.wait_ge(sems[s], v)
        self.q[eng_name].append(play)

    def playback(self, block):
        for name in self.ENGINES:
            ops = self.q[name]
            if not ops:
                continue

            def body(eng, ops=ops):
                for f in ops:
                    f(eng)
            getattr(block, name)(body)



HT_GROUPS = ((0, 8), (8, 15), (15, 22))


def emit_ffn(em, nc, st, tag, hT, p1, p3, po, xn_dram, w1_dram, w3_dram,
             w2_dram, consume, x_ready, stage2_hook=None, w13_init=()):
    from contextlib import ExitStack
    s1 = ExitStack()
    w1sb = s1.enter_context(nc.sbuf_tensor(tag + "w1sb", [128, 16 * 1024], BF16))
    w3sb = s1.enter_context(nc.sbuf_tensor(tag + "w3sb", [128, 16 * 1024], BF16))
    xsb = s1.enter_context(nc.sbuf_tensor(tag + "xsb", [128, 16 * 1024], BF16))
    silu_sb = s1.enter_context(nc.sbuf_tensor(tag + "silu", [128, 1024], F32))
    xcons = []
    w13cons = list(w13_init)
    p1_rd = None
    p3_rd = None
    for (h0, h1) in HT_GROUPS:
        nht = h1 - h0
        t_w1 = em.dma("gpsimd",
                      w1sb[:].rearrange("p (dt k) -> p dt k", dt=16)[:, :, 0:nht * 128],
                      w1_dram[:, h0 * 128:h1 * 128].rearrange("(dt p) h -> p dt h", p=128),
                      waits=w13cons, sem="wq")
        t_w3 = em.dma("gpsimd",
                      w3sb[:].rearrange("p (dt k) -> p dt k", dt=16)[:, :, 0:nht * 128],
                      w3_dram[:, h0 * 128:h1 * 128].rearrange("(dt p) h -> p dt h", p=128),
                      sem="wq")
        for sh in range(2):
            xt_toks = []
            for dt in range(16):
                xt_toks.append(em.dmat(
                    "sync", xsb[:, dt * 1024:(dt + 1) * 1024],
                    xn_dram[sh * 1024:(sh + 1) * 1024, dt * 128:(dt + 1) * 128],
                    waits=(x_ready + xcons) if dt == 0 else ()))
            xcons = []
            for ht in range(h0, h1):
                hr = ht - h0
                for (psum, wsb, t_w, rd) in ((p1, w1sb, t_w1, p1_rd),
                                             (p3, w3sb, t_w3, p3_rd)):
                    first_w = [t_w, xt_toks[-1]] + ([rd] if rd else [])
                    for dt in range(16):
                        for nk in range(2):
                            em.t("matmul",
                                 psum[:, nk * 512:(nk + 1) * 512],
                                 wsb[:, dt * 1024 + hr * 128:dt * 1024 + hr * 128 + 128],
                                 xsb[:, dt * 1024 + nk * 512:dt * 1024 + (nk + 1) * 512],
                                 start=(dt == 0), stop=(dt == 15),
                                 last=(dt == 15 and nk == 1),
                                 waits=first_w if (dt == 0 and nk == 0) else ())
                    first_w = []
                tok_p = em.tok("pe")
                tok_silu = em.s("activation", silu_sb[:], p1[:], AF.Silu,
                                waits=[tok_p])
                tok_h = em.v("tensor_tensor",
                             hT[:, ht * 2048 + sh * 1024: ht * 2048 + (sh + 1) * 1024],
                             p3[:], silu_sb[:], ALU.mult,
                             waits=[tok_silu])
                p1_rd, p3_rd = tok_silu, tok_h
            xcons = [em.tok("pe")]
        w13cons = [em.tok("pe")]
    hT_done = [em.tok("act"), em.tok("dve")]
    stage1_pe = em.tok("pe")
    s1.close()
    w2sb = st.enter_context(nc.sbuf_tensor(tag + "w2sb", [128, 22 * 2048], BF16))
    if stage2_hook is not None:
        stage2_hook()
    t_w2 = em.dma("gpsimd",
                  w2sb[:].rearrange("p (ht d) -> p ht d", ht=22),
                  w2_dram.rearrange("(ht p) d -> p ht d", p=128),
                  waits=[stage1_pe, em.tok("sq"), em.tok("aq"), em.tok("act")],
                  sem="wq")

    cons_toks = []
    for ct in range(16):
        first = True
        for ht in range(22):
            for nk in range(4):
                w = []
                if first:
                    w = hT_done + [t_w2] + cons_toks
                    first = False
                em.t("matmul", po[:, nk * 512:(nk + 1) * 512],
                     hT[:, ht * 2048 + ct * 128: ht * 2048 + ct * 128 + 128],
                     w2sb[:, ht * 2048 + nk * 512: ht * 2048 + (nk + 1) * 512],
                     start=(ht == 0), stop=(ht == 21),
                     last=(ht == 21 and nk == 3), waits=w)
        cons_toks = consume(ct, po, em.tok("pe"))



def build_l2():
    nc = bass.Bass("TRN2", target_bir_lowering=False, debug=False)
    h2s = nc.dram_tensor("h2s", [NE * LCAP, E], F32, kind="ExternalInput").ap()
    cnt_in = nc.dram_tensor("cnt", [8, 1], F32, kind="ExternalInput").ap()
    piota = nc.dram_tensor("piota", [128, 16], F32, kind="ExternalInput").ap()
    eoff = nc.dram_tensor("eoff", [128, 1], F32, kind="ExternalInput").ap()
    stri9 = nc.dram_tensor("stri9", [8, 9], F32, kind="ExternalInput").ap()
    ones1x128 = nc.dram_tensor("ones1x128", [1, 128], F32, kind="ExternalInput").ap()
    zrows = nc.dram_tensor("zrows", [128, E], F32, kind="ExternalInput").ap()
    w1 = nc.dram_tensor("w1", [E, HF], F32, kind="ExternalInput").ap()
    w3 = nc.dram_tensor("w3", [E, HF], F32, kind="ExternalInput").ap()
    w2 = nc.dram_tensor("w2", [HF, E], F32, kind="ExternalInput").ap()
    oe_out = nc.dram_tensor("oe", [CAP, E], F32, kind="ExternalOutput").ap()
    xd_dram = nc.dram_tensor("xd", [CAP, E], BF16, kind="Internal").ap()

    with ExitStack() as st:
        def sb(name, shape, dt=F32):
            return st.enter_context(nc.sbuf_tensor(name, shape, dt))

        def ps(name, shape, dt=F32):
            return st.enter_context(nc.psum_tensor(name, shape, dt))

        em = Em(nc)
        em.sem(st, "sq", "aq", "gq", "wq", "pe", "act", "dve", "pool")

        cnt_sb = sb("cnt_sb", [8, 1])
        stri_sb = sb("stri_sb", [8, 9])
        ones_sb = sb("ones_sb", [1, 128])
        piota_sb = sb("piota_sb", [128, 16])
        eoff_sb = sb("eoff_sb", [128, 1])
        cum9_sb = sb("cum9_sb", [1, 9])
        cumb_sb = sb("cumb_sb", [128, 9])
        accS = sb("accS", [128, 16])
        cumsel = sb("cumsel", [128, 16])
        ge = sb("ge", [128, 16])
        geN = sb("geN", [128, 16])
        tmp16 = sb("tmp16", [128, 16])
        rowf = sb("rowf", [128, 16])
        idx32 = sb("idx32", [128, 16], I32)
        hT = sb("hT", [128, 22 * 2048], BF16)
        oe_sb = sb("oe_sb", [128, E])
        oe_sb2 = sb("oe_sb2", [128, E])
        zscope = ExitStack()
        zrow = zscope.enter_context(nc.sbuf_tensor("zrow", [128, E], F32))
        zrow2 = zscope.enter_context(nc.sbuf_tensor("zrow2", [128, E], F32))

        from contextlib import ExitStack as _ES
        ps_scope = _ES()
        p_r = ps_scope.enter_context(nc.psum_tensor("p_r", [1, 9], F32))
        p_b = ps_scope.enter_context(nc.psum_tensor("p_b", [128, 9], F32))

        block = st.enter_context(nc.Block())

        t_cnt = em.dma("sync", cnt_sb[:], cnt_in)
        t_stri = em.dma("sync", stri_sb[:], stri9)
        t_ones = em.dma("sync", ones_sb[:], ones1x128)
        t_pio = em.dma("sync", piota_sb[:], piota)
        t_eof = em.dma("sync", eoff_sb[:], eoff)

        tok = em.t("matmul", p_r[:], cnt_sb[:], stri_sb[:], start=True, stop=True,
                   waits=[t_cnt, t_stri])
        tok_c9 = em.v("tensor_copy", cum9_sb[:], p_r[:], waits=[tok])
        tokb = em.t("matmul", p_b[:], ones_sb[:], cum9_sb[:], start=True, stop=True,
                    waits=[tok_c9, t_ones])
        tok_cb = em.v("tensor_copy", cumb_sb[:], p_b[:], waits=[tokb])

        em.v("tensor_scalar", accS[:], piota_sb[:], 0.0, None, ALU.mult,
             waits=[t_pio])
        em.v("tensor_scalar", cumsel[:], piota_sb[:], 0.0, None, ALU.mult)
        for s in range(1, 9):
            dst = geN if s == 8 else ge
            em.v("tensor_scalar", dst[:], piota_sb[:], cumb_sb[:, s:s + 1], None,
                 ALU.is_ge, waits=[tok_cb, t_pio])
            em.v("tensor_tensor", accS[:], accS[:], dst[:], ALU.add)
            em.v("tensor_scalar", tmp16[:], dst[:], cumb_sb[:, s:s + 1], None,
                 ALU.mult)
            em.v("tensor_tensor", cumsel[:], cumsel[:], tmp16[:], ALU.max)
        em.v("tensor_scalar", accS[:], accS[:], float(LCAP), None, ALU.mult)
        em.v("tensor_tensor", rowf[:], piota_sb[:], cumsel[:], ALU.subtract)
        em.v("tensor_tensor", rowf[:], rowf[:], accS[:], ALU.add)
        em.v("tensor_scalar", rowf[:], rowf[:], eoff_sb[:, 0:1], None, ALU.add,
             waits=[t_eof])
        em.v("tensor_scalar", tmp16[:], geN[:], 65536.0, None, ALU.mult)
        em.v("tensor_tensor", rowf[:], rowf[:], tmp16[:], ALU.add)
        tok_idx = em.v("tensor_copy", idx32[:], rowf[:])

        gts = []
        st_tok = None
        tz0 = em.dma("sync", zrow[:], zrows)
        tz1 = em.dma("sync", zrow2[:], zrows)
        for j in range(16):
            buf = zrow if j % 2 == 0 else zrow2
            w = [tok_idx, tz0, tz1]
            if st_tok is not None:
                w.append(st_tok)
            if j >= 2:
                tok_z = em.dma("sync", buf[:], zrows, waits=w)
                w = [tok_z]
            gt = em.gather(buf[:], h2s, idx32[:, j:j + 1], NE * LCAP - 1,
                           waits=w)
            st_tok = em.dma("gpsimd", xd_dram[j * 128:(j + 1) * 128, :], buf[:],
                            waits=[gt])
            gts.append(st_tok)
        x_ready = [gts[-1]]

        def consume(ct, po_ap, pe_tok):
            buf = oe_sb if ct % 2 == 0 else oe_sb2
            tok_e = em.v("tensor_copy", buf[:], po_ap[:], waits=[pe_tok])
            tok_o = em.dma("scalar", oe_out[ct * 128:(ct + 1) * 128, :], buf[:],
                           waits=[tok_e])
            return [tok_e, tok_o]

        ps_scope.close()
        zscope.close()
        p1 = ps("p1", [128, 1024])
        p3 = ps("p3", [128, 1024])
        po = ps("po", [128, 2048])
        emit_ffn(em, nc, st, "f_", hT, p1, p3, po,
                 xd_dram, w1, w3, w2, consume, x_ready, w13_init=x_ready)

        em.finish("sync")
        em.playback(block)
    return nc



def _bf(x):
    return np.ascontiguousarray(x.astype(BF) if x.dtype != BF else x)


def l2_in_maps(h2sorted_all, counts_all, w1, w3, w2):
    piota = (np.arange(128)[:, None] + 128 * np.arange(16)[None, :]).astype(np.float32)
    stri9 = np.zeros((8, 9), np.float32)
    for k in range(8):
        for s in range(9):
            stri9[k, s] = 1.0 if k < s else 0.0
    ones1x128 = np.ones((1, 128), np.float32)
    zrows = np.zeros((128, E), np.float32)
    maps = []
    for e in range(NC):
        h2s_e = np.ascontiguousarray(np.concatenate(
            [np.asarray(h2sorted_all[s][e * LCAP:(e + 1) * LCAP], np.float32)
             for s in range(NC)], axis=0))
        maps.append({
            "h2s": h2s_e,
            "zrows": zrows,
            "cnt": np.ascontiguousarray(counts_all[:, e:e + 1]).astype(np.float32),
            "piota": piota,
            "eoff": np.zeros((128, 1), np.float32),
            "stri9": stri9,
            "ones1x128": ones1x128,
            "w1": np.ascontiguousarray(w1[e]),
            "w3": np.ascontiguousarray(w3[e]),
            "w2": np.ascontiguousarray(w2[e]),
        })
    return maps



def build_l3():
    nc = bass.Bass("TRN2", target_bir_lowering=False, debug=False)
    oeall = nc.dram_tensor("oeall", [NE * CAP, E], F32, kind="ExternalInput").ap()
    h2own = nc.dram_tensor("h2own", [NT, E], F32, kind="ExternalInput").ap()
    h2b = nc.dram_tensor("h2b", [NT, E], BF16, kind="Internal").ap()
    xmid = nc.dram_tensor("xmid", [NT, E], F32, kind="ExternalInput").ap()
    m_eidx = nc.dram_tensor("m_eidx", [128, 16], F32, kind="ExternalInput").ap()
    m_gate = nc.dram_tensor("m_gate", [128, 16], F32, kind="ExternalInput").ap()
    m_lpos = nc.dram_tensor("m_lpos", [128, 16], F32, kind="ExternalInput").ap()
    cnts = nc.dram_tensor("cnts", [8, 8], F32, kind="ExternalInput").ap()
    rowsel = nc.dram_tensor("rowsel", [8, 1], F32, kind="ExternalInput").ap()
    stri8 = nc.dram_tensor("stri8", [8, 8], F32, kind="ExternalInput").ap()
    ones1x128 = nc.dram_tensor("ones1x128", [1, 128], F32, kind="ExternalInput").ap()
    sw1 = nc.dram_tensor("sw1", [E, HF], F32, kind="ExternalInput").ap()
    sw3 = nc.dram_tensor("sw3", [E, HF], F32, kind="ExternalInput").ap()
    sw2 = nc.dram_tensor("sw2", [HF, E], F32, kind="ExternalInput").ap()
    out = nc.dram_tensor("out", [NT, E], F32, kind="ExternalOutput").ap()

    with ExitStack() as st:
        def sb(name, shape, dt=F32):
            return st.enter_context(nc.sbuf_tensor(name, shape, dt))

        def ps(name, shape, dt=F32):
            return st.enter_context(nc.psum_tensor(name, shape, dt))

        em = Em(nc)
        em.sem(st, "sq", "aq", "gq", "wq", "pe", "act", "dve", "pool")

        cnts_sb = sb("cnts_sb", [8, 8])
        rsel_sb = sb("rsel_sb", [8, 1])
        stri_sb = sb("stri_sb", [8, 8])
        ones_sb = sb("ones_sb", [1, 128])
        bp_sb = sb("bp_sb", [8, 8])
        myrow_sb = sb("myrow_sb", [1, 8])
        bpb_sb = sb("bpb_sb", [128, 8])
        eidx_sb = sb("eidx_sb", [128, 16])
        gate_sb = sb("gate_sb", [128, 16])
        lpos_sb = sb("lpos_sb", [128, 16])
        acc_sb = sb("acc_sb", [128, 16])
        tmp_sb = sb("tmp_sb", [128, 16])
        keep_sb = sb("keep_sb", [128, 16])
        gk_sb = sb("gk_sb", [128, 16])
        flat_sb = sb("flat_sb", [128, 16])
        idx32 = sb("idx32", [128, 16], I32)
        combine = {}

        from contextlib import ExitStack as _ES
        ps_scope = _ES()
        p_r8 = ps_scope.enter_context(nc.psum_tensor("p_r8", [8, 8], F32))
        p_m = ps_scope.enter_context(nc.psum_tensor("p_m", [1, 8], F32))
        p_bb = ps_scope.enter_context(nc.psum_tensor("p_bb", [128, 8], F32))

        block = st.enter_context(nc.Block())

        t_c = em.dma("sync", cnts_sb[:], cnts)
        t_r = em.dma("sync", rsel_sb[:], rowsel)
        t_s = em.dma("sync", stri_sb[:], stri8)
        t_o = em.dma("sync", ones_sb[:], ones1x128)
        t_e = em.dma("sync", eidx_sb[:], m_eidx)
        t_g = em.dma("sync", gate_sb[:], m_gate)
        t_l = em.dma("sync", lpos_sb[:], m_lpos)

        tok = em.t("matmul", p_r8[:], stri_sb[:], cnts_sb[:], start=True, stop=True,
                   waits=[t_c, t_s])
        tok_bp = em.v("tensor_copy", bp_sb[:], p_r8[:], waits=[tok])
        tok = em.t("matmul", p_m[:], rsel_sb[:], bp_sb[:], start=True, stop=True,
                   waits=[tok_bp, t_r])
        tok_mr = em.v("tensor_copy", myrow_sb[:], p_m[:], waits=[tok])
        tok = em.t("matmul", p_bb[:], ones_sb[:], myrow_sb[:], start=True, stop=True,
                   waits=[tok_mr, t_o])
        tok_bb = em.v("tensor_copy", bpb_sb[:], p_bb[:], waits=[tok])

        em.v("tensor_scalar", acc_sb[:], eidx_sb[:], 0.0, None, ALU.mult,
             waits=[t_e])
        for e in range(8):
            em.v("tensor_scalar", tmp_sb[:], eidx_sb[:], float(e), None,
                 ALU.is_equal, waits=[t_e])
            em.v("tensor_scalar", tmp_sb[:], tmp_sb[:], bpb_sb[:, e:e + 1], None,
                 ALU.mult, waits=[tok_bb])
            em.v("tensor_tensor", acc_sb[:], acc_sb[:], tmp_sb[:], ALU.add)
        em.v("tensor_tensor", acc_sb[:], acc_sb[:], lpos_sb[:], ALU.add,
             waits=[t_l])
        em.v("tensor_scalar", keep_sb[:], acc_sb[:], 2047.5, None, ALU.is_le)
        em.v("tensor_scalar", flat_sb[:], eidx_sb[:], float(CAP), None, ALU.mult)
        em.v("tensor_tensor", flat_sb[:], flat_sb[:], acc_sb[:], ALU.add)
        em.v("tensor_scalar", flat_sb[:], flat_sb[:], float(NE * CAP - 1), None,
             ALU.min)
        em.v("tensor_tensor", gk_sb[:], gate_sb[:], keep_sb[:], ALU.mult,
             waits=[t_g])
        tok_idx = em.v("tensor_copy", idx32[:], flat_sb[:])

        ps_scope.close()
        p1 = ps("p1", [128, 1024])
        p3 = ps("p3", [128, 1024])
        po = ps("po", [128, 2048])

        pre = {}
        done_toks = {}

        def stage2_hook():
            combine["oeg"] = [sb("oeg0", [128, E]), sb("oeg1", [128, E])]
            combine["xms"] = [sb("xms0", [128, E], BF16), sb("xms1", [128, E], BF16)]

        def prefetch(ct, extra_waits):
            w = extra_waits + [em.tok("pe"), em.tok("act"), em.tok("dve")]
            pre[ct] = (
                em.gather(combine["oeg"][ct % 2][:], oeall, idx32[:, ct:ct + 1],
                          NE * CAP - 1, waits=[tok_idx] + w),
                em.dma("gpsimd", combine["xms"][ct % 2][:],
                       xmid[ct * 128:(ct + 1) * 128, :], waits=w),
            )

        def consume(ct, po_ap, pe_tok):
            if ct == 0:
                prefetch(0, [])
                prefetch(1, [])
            g_tok, x_tok = pre[ct]
            acc = combine["oeg"][ct % 2]
            t1 = em.v("tensor_scalar", acc[:], acc[:],
                      gk_sb[:, ct:ct + 1], None, ALU.mult, waits=[g_tok])
            t2 = em.v("tensor_tensor", acc[:], acc[:], po_ap[:], ALU.add,
                      waits=[pe_tok])
            t3 = em.v("tensor_tensor", acc[:], acc[:],
                      combine["xms"][ct % 2][:], ALU.add, waits=[x_tok])
            t4 = em.dma("scalar", out[ct * 128:(ct + 1) * 128, :],
                        acc[:], waits=[t3])
            done_toks[ct] = t4
            if ct + 2 < 16:
                prefetch(ct + 2, [t3, t4])
            return [t2]

        t_h2b = em.dma("gpsimd", h2b, h2own, waits=[])
        emit_ffn(em, nc, st, "s_", sb("hT", [128, 22 * 2048], BF16),
                 p1, p3, po, h2b, sw1, sw3, sw2, consume, [tok_idx, t_h2b],
                 stage2_hook=stage2_hook)

        em.finish("sync")
        em.playback(block)
    return nc


def l3_in_maps(oe_blocks, h2own_l, xmid_l, meta_l, counts_all, sw1, sw3, sw2):
    oeall = np.ascontiguousarray(np.concatenate(oe_blocks, axis=0))
    stri8 = np.zeros((8, 8), np.float32)
    for k in range(8):
        for s in range(8):
            stri8[k, s] = 1.0 if k < s else 0.0
    ones1x128 = np.ones((1, 128), np.float32)
    maps = []
    for c in range(NC):
        rowsel = np.zeros((8, 1), np.float32)
        rowsel[c, 0] = 1.0
        maps.append({
            "oeall": oeall,
            "h2own": h2own_l[c],
            "xmid": xmid_l[c],
            "m_eidx": meta_l[c][0], "m_gate": meta_l[c][1], "m_lpos": meta_l[c][2],
            "cnts": counts_all.astype(np.float32),
            "rowsel": rowsel,
            "stri8": stri8,
            "ones1x128": ones1x128,
            "sw1": np.ascontiguousarray(sw1),
            "sw3": np.ascontiguousarray(sw3),
            "sw2": np.ascontiguousarray(sw2),
        })
    return maps


MAGIC = 19450816


def build_l1(dev_stub=False, dev_outputs=False):
    nc = bass.Bass("TRN2", target_bir_lowering=False, debug=False,
                   detect_race_conditions=dev_stub)
    din = lambda n, s, dt=F32: nc.dram_tensor(n, s, dt, kind="ExternalInput").ap()
    dout = lambda n, s, dt=F32: nc.dram_tensor(n, s, dt, kind="ExternalOutput").ap()
    dint = lambda n, s, dt=F32: nc.dram_tensor(
        n, s, dt, kind="ExternalOutput" if dev_outputs else "Internal").ap()

    xb = din("xb", [T, E])
    xown = din("xown", [NT, E])
    wqT = din("wqT", [E, 1024]); wkT = din("wkT", [E, 1024]); wvT = din("wvT", [E, 1024])
    woT = din("woT", [1024, E])
    bq = din("bq", [128, 8]); bk = din("bk", [128, 8])
    bvrow = din("bvrow", [1, 1024])
    bocol = din("bocol", [128, 16])
    wg = din("wg", [E, 8])
    cosd = din("cosd", [128, T]); sind = din("sind", [128, T])
    dmaskd = din("dmaskd", [128, 2048])
    ones128b = din("ones128b", [128, 1], BF16)
    ones1x128 = din("ones1x128", [1, 128])
    iotam = din("iotam", [128, 8])
    iota8b = din("iota8b", [128, 8])
    ut128 = din("ut128", [128, 128])
    stri16 = din("stri16", [16, 16])
    ones16 = din("ones16", [16, 1])
    e127 = din("e127", [128, 1])
    idnf = din("idnf", [128, 128])
    idnb = din("idnb", [128, 128], BF16)
    epsin = din("epsin", [128, 1])
    sclin = din("sclin", [128, 1])
    zrows = din("zrows", [128, E])
    fidxd = din("fidxd", [128, 128], I32)
    pidxd = din("pidxd", [128, 64], I32)
    midxd = din("midxd", [128, 64], I32)
    plidxd = din("plidxd", [2, 1], I32)
    flidxd = din("flidxd", [2, 1], I32)

    xmid_o = dout("xmid", [NT, E])
    h2o_o = dout("h2o", [NT, E])
    h2s_o = dout("h2s", [NE * LCAP, E])
    cnt_o = dout("cnt", [1, 8])
    meidx_o = dout("m_eidx", [128, 16])
    mgate_o = dout("m_gate", [128, 16])
    mlpos_o = dout("m_lpos", [128, 16])

    lnN = dint("lnN", [T, E], BF16)
    h2f = nc.dram_tensor("h2f", [NT, E], F32, kind="Internal").ap()
    qTd = dint("qTd", [1024, T], BF16)
    kTd = dint("kTd", [1024, T], BF16)
    vNd = dint("vNd", [T, 1024], BF16)
    yTd = dint("yTd", [1024, T], BF16)
    bnc = nc.dram_tensor("bnc", [1, 128], F32, kind="Internal").ap()
    bnc2 = nc.dram_tensor("bnc2", [16, 8], F32, kind="Internal").ap()
    if dev_stub:
        shrd = din("pp", [2 * 16384, LCAP], I32)
        shwr = nc.dram_tensor("shsc", [2 * 16384, LCAP], I32, kind="Internal").ap()
        shflag = nc.dram_tensor("shflag", [4, 64], I32, kind="Internal").ap()
    else:
        shrd = nc.dram_tensor("shpair", [2 * 16384, LCAP], I32, kind="Internal",
                              addr_space="Shared").ap()
        shwr = shrd
        shflag = nc.dram_tensor("shflag", [4, 64], I32, kind="Internal",
                                addr_space="Shared").ap()

    with ExitStack() as st:
        def sb(name, shape, dt=F32):
            return st.enter_context(nc.sbuf_tensor(name, shape, dt))

        em = Em(nc)
        em.sem(st, "sq", "aq", "gq", "wq", "pq", "pe", "act", "dve", "pool")

        cos_sb = sb("cos_sb", [128, T]); sin_sb = sb("sin_sb", [128, T])
        dmask_sb = sb("dmask_sb", [128, 2048])
        bq_sb = sb("bq_sb", [128, 8]); bk_sb = sb("bk_sb", [128, 8])
        bvb_sb = sb("bvb_sb", [128, 1024])
        bo_sb = sb("bo_sb", [128, 16])
        o128b_sb = sb("o128b_sb", [128, 1], BF16)
        o1x_sb = sb("o1x_sb", [1, 128])
        iotam_sb = sb("iotam_sb", [128, 8]); iota8_sb = sb("iota8_sb", [128, 8])
        ut_sb = sb("ut_sb", [128, 128])
        stri16_sb = sb("stri16_sb", [16, 16]); ones16_sb = sb("ones16_sb", [16, 1])
        e127_sb = sb("e127_sb", [128, 1])
        idnf_sb = sb("idnf_sb", [128, 128])
        idnb_sb = sb("idnb_sb", [128, 128], BF16)
        wg_sb = sb("wg_sb", [128, 16 * 8], BF16)
        fidx_sb = sb("fidx_sb", [128, 128], I32)
        pidx_sb = sb("pidx_sb", [128, 64], I32)
        midx_sb = sb("midx_sb", [128, 64], I32)
        plidx_sb = sb("plidx_sb", [2, 1], I32)
        flidx_sb = sb("flidx_sb", [2, 1], I32)
        flag_sb = sb("flag_sb", [2, 64], I32)
        pfl_sb = sb("pfl_sb", [2, 64], I32)
        meidx_sb = sb("meidx_sb", [128, 16])
        eps_sb = sb("eps_sb", [128, 1])
        scl_sb = sb("scl_sb", [128, 1])
        mgate_sb = sb("mgate_sb", [128, 16])
        mlpos_sb = sb("mlpos_sb", [128, 16])

        block = st.enter_context(nc.Block())

        t_cos = em.dma("sync", cos_sb[:], cosd)
        t_sin = em.dma("sync", sin_sb[:], sind)
        em.dma("sync", dmask_sb[:], dmaskd)
        t_bq = em.dma("sync", bq_sb[:], bq)
        t_bk = em.dma("sync", bk_sb[:], bk)
        t_bo = em.dma("sync", bo_sb[:], bocol)
        t_o128 = em.dma("sync", o128b_sb[:], ones128b)
        t_o1x = em.dma("sync", o1x_sb[:], ones1x128)
        t_iom = em.dma("sync", iotam_sb[:], iotam)
        t_io8 = em.dma("sync", iota8_sb[:], iota8b)
        t_ut = em.dma("sync", ut_sb[:], ut128)
        t_s16 = em.dma("sync", stri16_sb[:], stri16)
        t_o16 = em.dma("sync", ones16_sb[:], ones16)
        t_e127 = em.dma("sync", e127_sb[:], e127)
        t_idf = em.dma("sync", idnf_sb[:], idnf)
        t_idb = em.dma("sync", idnb_sb[:], idnb)
        t_wg = em.dma("gpsimd", wg_sb[:].rearrange("p (et j) -> p et j", et=16),
                      wg.rearrange("(et p) j -> p et j", p=128), sem="wq")
        em.dma("sync", fidx_sb[:], fidxd)
        em.dma("sync", pidx_sb[:], pidxd)
        em.dma("sync", midx_sb[:], midxd)
        em.dma("sync", plidx_sb[:], plidxd)
        em.dma("sync", flidx_sb[:], flidxd)
        tokEPS = em.dma("sync", eps_sb[:], epsin)
        tokSCL = em.dma("sync", scl_sb[:], sclin)
        consts = [em.tok("sq"), em.tok("wq")]

        red = sb("red", [128, 1]); red2 = sb("red2", [128, 1])
        mmean = sb("mmean", [128, 1]); rstd = sb("rstd", [128, 1])
        sA = ExitStack()
        xch = [sA.enter_context(nc.sbuf_tensor(f"xch{i}", [128, E], F32))
               for i in range(2)]
        lnc = [sA.enter_context(nc.sbuf_tensor(f"lnc{i}", [128, E], BF16))
               for i in range(2)]
        sqv = sA.enter_context(nc.sbuf_tensor("sqv", [128, E], F32))

        def emit_ln(src_ap, dst_ap, waits):
            em.v("reduce_sum", red[:], src_ap, axis=AX.X, waits=waits)
            tokQ = em.s("activation", sqv[:], src_ap, AF.Square, waits=waits)
            em.v("reduce_sum", red2[:], sqv[:], axis=AX.X, waits=[tokQ])
            em.v("tensor_scalar", mmean[:], red[:], 1.0 / E, None, ALU.mult)
            em.v("tensor_scalar", red2[:], red2[:], 1.0 / E, None, ALU.mult)
            em.v("tensor_tensor", red[:], mmean[:], mmean[:], ALU.mult)
            tokV = em.v("tensor_tensor", red2[:], red2[:], red[:], ALU.subtract)
            tokSD = em.s("activation", rstd[:], red2[:], AF.Sqrt,
                         bias=eps_sb[:, 0:1], waits=[tokV, tokEPS])
            tokRI = em.v("reciprocal", rstd[:], rstd[:], waits=[tokSD])
            return em.v("tensor_scalar", dst_ap, src_ap, mmean[:], rstd[:],
                        ALU.subtract, ALU.mult, waits=[tokRI])

        lnw = {}
        for tc in range(32):
            buf = xch[tc % 2]
            t_x = em.dma("sync", buf[:], xb[tc * 128:(tc + 1) * 128, :],
                         waits=[lnw.get(("n", tc % 2), ("dve", 0))])
            tokN = emit_ln(buf[:], lnc[tc % 2][:],
                           [t_x, lnw.get(("s", tc % 2), ("aq", 0))])
            t_st = em.dma("scalar", lnN[tc * 128:(tc + 1) * 128, :],
                          lnc[tc % 2][:], waits=[tokN])
            lnw[("n", tc % 2)] = tokN
            lnw[("s", tc % 2)] = t_st
        lnN_ready = [em.tok("aq")]
        sA.close()
        em.phase()

        sB = ExitStack()
        wsb = sB.enter_context(nc.sbuf_tensor("wsb", [128, 16 * 1024], BF16))
        lnsb = [sB.enter_context(nc.sbuf_tensor(f"lnsb{i}", [128, 16 * 512], BF16))
                for i in range(2)]
        ev_f = sB.enter_context(nc.sbuf_tensor("ev_f", [128, 512], F32))
        rot_f = sB.enter_context(nc.sbuf_tensor("rot_f", [128, 512], F32))
        c1_f = sB.enter_context(nc.sbuf_tensor("c1_f", [128, 512], F32))
        c2_f = sB.enter_context(nc.sbuf_tensor("c2_f", [128, 512], F32))
        stg = [sB.enter_context(nc.sbuf_tensor(f"stg{i}", [128, 512], BF16))
               for i in range(2)]
        vstg = [sB.enter_context(nc.sbuf_tensor(f"vstg{i}", [128, 1024], BF16))
                for i in range(2)]
        psB = ExitStack()
        p_pr = psB.enter_context(nc.psum_tensor("p_pr", [128, 4096], F32))

        def qk_pass(wT_dram, b_sb, b_tok, out_dram, tag):
            t_w = em.dma("gpsimd", wsb[:].rearrange("p (et f) -> p et f", et=16),
                         wT_dram.rearrange("(et p) f -> p et f", p=128),
                         waits=em.phase_toks(), sem="wq")
            last = {("ln", 0): em.tok("pe"), ("ln", 1): em.tok("pe"),
                    ("st", 0): em.tok("aq"), ("st", 1): em.tok("aq"),
                    ("ev", 0): em.tok("dve"), ("ev", 1): em.tok("dve"),
                    ("evs", 0): em.tok("aq"), ("evs", 1): em.tok("aq"),
                    "evf": em.tok("dve"), "evfp": em.tok("pool"),
                    "rot": em.tok("pool"), "c1": em.tok("dve"),
                    "c2": em.tok("dve")}
            for tc in range(8):
                lbuf = lnsb[tc % 2]
                tl = None
                for et in range(16):
                    tl = em.dmat("sync", lbuf[:, et * 512:(et + 1) * 512],
                                 lnN[tc * 512:(tc + 1) * 512,
                                     et * 128:(et + 1) * 128],
                                 waits=(lnN_ready +
                                        [last.get(("ln", tc % 2), ("pe", 0))])
                                 if et == 0 else ())
                for jg in range(2):
                    for et in range(16):
                        for j in range(jg * 4, jg * 4 + 4):
                            w = ()
                            if et == 0 and j == jg * 4:
                                w = [t_w, tl, last.get(("ev", jg), ("dve", 0)),
                                     last.get(("evs", jg), ("aq", 0))]
                            em.t("matmul", p_pr[:, j * 512:(j + 1) * 512],
                                 wsb[:, et * 1024 + j * 128:et * 1024 + (j + 1) * 128],
                                 lbuf[:, et * 512:(et + 1) * 512],
                                 start=(et == 0), stop=(et == 15),
                                 last=(et == 15 and j == jg * 4 + 3), waits=w)
                    mm_tok = em.tok("pe")
                    for j in range(jg * 4, jg * 4 + 4):
                        pr = p_pr[:, j * 512:(j + 1) * 512]
                        tokB = em.v("tensor_scalar", ev_f[:], pr,
                                    b_sb[:, j:j + 1], None, ALU.add,
                                    waits=[mm_tok, b_tok,
                                           last.get("evfp", ("pool", 0))])
                        tokR1 = em.v("tensor_scalar", rot_f[0:64, :],
                                     ev_f[64:128, :], -1.0, None, ALU.mult,
                                     waits=[tokB, last.get("rot", ("pool", 0))])
                        tokR2 = em.v("tensor_copy", rot_f[64:128, :],
                                     ev_f[0:64, :])
                        tokC1 = em.g("tensor_tensor", c1_f[:], ev_f[:],
                                     cos_sb[:, tc * 512:(tc + 1) * 512],
                                     ALU.mult, waits=[tokB, t_cos,
                                                      last.get("c1", ("dve", 0))])
                        tokC2 = em.g("tensor_tensor", c2_f[:], rot_f[:],
                                     sin_sb[:, tc * 512:(tc + 1) * 512],
                                     ALU.mult, waits=[tokR2, t_sin,
                                                      last.get("c2", ("dve", 0))])
                        tokS = em.v("tensor_tensor", stg[j % 2][:], c1_f[:],
                                    c2_f[:], ALU.add,
                                    waits=[tokC1, tokC2,
                                           last.get(("st", j % 2), ("aq", 0))])
                        t_out = em.dma("scalar",
                                       out_dram[j * 128:(j + 1) * 128,
                                                tc * 512:(tc + 1) * 512],
                                       stg[j % 2][:], waits=[tokS])
                        last["evf"] = tokC1
                        last["evfp"] = tokC1
                        last["rot"] = tokC2
                        last["c1"] = tokS
                        last["c2"] = tokS
                        last[("st", j % 2)] = t_out
                        last[("ev", jg)] = tokS
                        last[("evs", jg)] = t_out
                last[("ln", tc % 2)] = em.tok("pe")

        qk_pass(wqT, bq_sb, t_bq, qTd, "q")
        qk_pass(wkT, bk_sb, t_bk, kTd, "k")

        t_wv = em.dma("gpsimd", wsb[:].rearrange("p (et f) -> p et f", et=16),
                      wvT.rearrange("(et p) f -> p et f", p=128),
                      waits=em.phase_toks(), sem="wq")
        t_bv = em.dma("sync", bvb_sb[0:1, :], bvrow)
        tokBV = em.t("matmul", p_pr[:, 0:512], o1x_sb[:], bvb_sb[0:1, 0:512],
                     start=True, stop=True,
                     waits=[t_bv, t_o1x, em.tok("act"), em.tok("dve")])
        tokBV = em.t("matmul", p_pr[:, 512:1024], o1x_sb[:], bvb_sb[0:1, 512:1024],
                     start=True, stop=True)
        tokBVc = em.v("tensor_copy", bvb_sb[:], p_pr[:, 0:1024], waits=[tokBV])
        last = {("ln", 0): em.tok("pe"), ("ln", 1): em.tok("pe"),
                ("st", 0): em.tok("aq"), ("st", 1): em.tok("aq"),
                ("ev", 0): em.tok("dve"), ("ev", 1): em.tok("dve")}
        for tc in range(8):
            lbuf = lnsb[tc % 2]
            tl = None
            for et in range(16):
                tl = em.dmat("sync", lbuf[:, et * 512:(et + 1) * 512],
                             lnN[tc * 512:(tc + 1) * 512, et * 128:(et + 1) * 128],
                             waits=[last.get(("ln", tc % 2), ("pe", 0))]
                             if et == 0 else ())
            for tt in range(4):
                pv = p_pr[:, 1024 + (tt % 2) * 1024:1024 + (tt % 2) * 1024 + 1024]
                for et in range(16):
                    for nk in range(2):
                        w = ()
                        if et == 0 and nk == 0:
                            w = [t_wv, tl, tokBVc,
                                 last.get(("ev", tt % 2), ("dve", 0))]
                        em.t("matmul", pv[:, nk * 512:(nk + 1) * 512],
                             lbuf[:, et * 512 + tt * 128:et * 512 + (tt + 1) * 128],
                             wsb[:, et * 1024 + nk * 512:et * 1024 + (nk + 1) * 512],
                             start=(et == 0), stop=(et == 15),
                             last=(et == 15 and nk == 1), waits=w)
                mm_tok = em.tok("pe")
                tokV = em.v("tensor_tensor", vstg[tt % 2][:], pv[:], bvb_sb[:],
                            ALU.add, waits=[mm_tok,
                                            last.get(("st", tt % 2), ("aq", 0))])
                t_outv = em.dma("scalar",
                                vNd[tc * 512 + tt * 128:tc * 512 + (tt + 1) * 128, :],
                                vstg[tt % 2][:], waits=[tokV])
                last[("ev", tt % 2)] = tokV
                last[("st", tt % 2)] = t_outv
            last[("ln", tc % 2)] = em.tok("pe")
        qkv_done = [em.tok("aq"), em.tok("pe")]
        psB.close()
        sB.close()
        em.phase()

        sC = ExitStack()
        kh_sb = sC.enter_context(nc.sbuf_tensor("kh_sb", [128, T], BF16))
        qh_sb = sC.enter_context(nc.sbuf_tensor("qh_sb", [128, T], BF16))
        vh_sb = sC.enter_context(nc.sbuf_tensor("vh_sb", [128, T], BF16))
        exps = [sC.enter_context(nc.sbuf_tensor(f"exps{i}", [128, 1024], BF16))
                for i in range(2)]
        rec_sb = sC.enter_context(nc.sbuf_tensor("rec_sb", [1, 512], F32))
        bcf_sb = sC.enter_context(nc.sbuf_tensor("bcf_sb", [128, 512], F32))
        ystg = [sC.enter_context(nc.sbuf_tensor(f"ystg{i}", [128, 512], BF16))
                for i in range(2)]
        psC = ExitStack()
        psS = [psC.enter_context(nc.psum_tensor(f"psS{i}", [128, 1024], F32))
               for i in range(2)]
        psY = psC.enter_context(nc.psum_tensor("psY", [128, 512], F32))
        psD = psC.enter_context(nc.psum_tensor("psD", [1, 512], F32))
        psBC = psC.enter_context(nc.psum_tensor("psBC", [128, 512], F32))

        last = {}
        for hj in range(8):
            t_k = em.dma("sync", kh_sb[:], kTd[hj * 128:(hj + 1) * 128, :],
                         waits=(em.phase_toks() if hj == 0 else []) +
                         [last.get("khf", ("pe", 0))])
            t_q = em.dma("sync", qh_sb[:], qTd[hj * 128:(hj + 1) * 128, :])
            t_v = em.dma("sync",
                         vh_sb[:].rearrange("p (tt d) -> p tt d", tt=32),
                         vNd[:, hj * 128:(hj + 1) * 128].rearrange(
                             "(tt p) d -> p tt d", p=128))
            for qc in range(8):
                npair = 2 * (qc + 1)
                for pb in range(npair):
                    ps = psS[pb % 2]
                    exb = exps[pb % 2]
                    w0 = [t_k, t_q, last.get(("exp", pb % 2), ("act", 0))]
                    for half in range(2):
                        kb = pb * 2 + half
                        em.t("matmul", ps[:, half * 512:(half + 1) * 512],
                             kh_sb[:, kb * 128:(kb + 1) * 128],
                             qh_sb[:, qc * 512:(qc + 1) * 512],
                             start=True, stop=True,
                             last=(half == 1), waits=w0 if half == 0 else ())
                    mm_tok = em.tok("pe")
                    mw = [mm_tok]
                    if pb >= npair - 2:
                        for half in range(2):
                            kb = pb * 2 + half
                            v = kb - qc * 4
                            if 0 <= v <= 3:
                                tokM = em.v("tensor_tensor",
                                            ps[:, half * 512:(half + 1) * 512],
                                            ps[:, half * 512:(half + 1) * 512],
                                            dmask_sb[:, v * 512:(v + 1) * 512],
                                            ALU.add, waits=[mm_tok])
                                mw = [tokM]
                    tokE = em.s("activation", exb[:], ps[:], AF.Exp,
                                scale=scl_sb[:, 0:1],
                                waits=mw + [tokSCL,
                                            last.get(("eu", pb % 2), ("pe", 0))])
                    last[("exp", pb % 2)] = tokE
                    for half in range(2):
                        is_last = (pb == npair - 1 and half == 1)
                        wD = [tokE]
                        wY = [tokE]
                        if pb == 0 and half == 0:
                            wD += [t_o128, last.get("recD", ("dve", 0))]
                            wY += [t_v, last.get("recY", ("dve", 0))]
                        em.t("matmul", psD[:],
                             o128b_sb[:], exb[:, half * 512:(half + 1) * 512],
                             start=(pb == 0 and half == 0), stop=is_last,
                             last=False, waits=wD)
                        kb = pb * 2 + half
                        em.t("matmul", psY[:],
                             vh_sb[:, kb * 128:(kb + 1) * 128],
                             exb[:, half * 512:(half + 1) * 512],
                             start=(pb == 0 and half == 0), stop=is_last,
                             last=is_last, waits=wY)
                    last[("eu", pb % 2)] = em.tok("pe")
                pe_tok = em.tok("pe")
                tokR = em.v("reciprocal", rec_sb[:], psD[:], waits=[pe_tok])
                last["recD"] = tokR
                tokBCm = em.t("matmul", psBC[:], o1x_sb[:], rec_sb[:],
                              start=True, stop=True, waits=[tokR, t_o1x])
                tokBC = em.v("tensor_copy", bcf_sb[:], psBC[:], waits=[tokBCm])
                tokYN = em.v("tensor_tensor", ystg[qc % 2][:], psY[:], bcf_sb[:],
                             ALU.mult,
                             waits=[tokBC, last.get(("yst", qc % 2), ("aq", 0))])
                last["recY"] = tokYN
                t_y = em.dma("scalar",
                             yTd[hj * 128:(hj + 1) * 128, qc * 512:(qc + 1) * 512],
                             ystg[qc % 2][:], waits=[tokYN])
                last[("yst", qc % 2)] = t_y
            last["khf"] = em.tok("pe")
        attn_done = [em.tok("aq")]
        psC.close()
        sC.close()
        em.phase()

        sD = ExitStack()
        wo_sb = sD.enter_context(nc.sbuf_tensor("wo_sb", [128, 8 * 2048], BF16))
        yTc = [sD.enter_context(nc.sbuf_tensor(f"yTc{i}", [128, 8 * 512], BF16))
               for i in range(2)]
        pstg = [sD.enter_context(nc.sbuf_tensor(f"pstg{i}", [128, 512], F32))
                for i in range(2)]
        psD2 = ExitStack()
        psO = [psD2.enter_context(nc.psum_tensor(f"psO{i}", [128, 512], F32))
               for i in range(2)]

        t_wo = em.dma("gpsimd", wo_sb[:].rearrange("p (et f) -> p et f", et=8),
                      woT.rearrange("(et p) f -> p et f", p=128),
                      waits=em.phase_toks(), sem="wq")
        last = {}
        nscat = 0
        for tc in range(8):
            t_y = em.dma("sync",
                         yTc[tc % 2][:].rearrange("p (et t) -> p et t", et=8),
                         yTd[:, tc * 512:(tc + 1) * 512].rearrange(
                             "(et p) t -> p et t", p=128),
                         waits=(em.phase_toks() if tc == 0 else []) +
                         [last.get(("yu", tc % 2), ("pe", 0))])
            for ft in range(16):
                po = psO[ft % 2]
                for et in range(8):
                    w = ()
                    if et == 0:
                        w = [t_wo, t_y, last.get(("ev", ft % 2), ("dve", 0))]
                    em.t("matmul", po[:],
                         wo_sb[:, et * 2048 + ft * 128:et * 2048 + (ft + 1) * 128],
                         yTc[tc % 2][:, et * 512:(et + 1) * 512],
                         start=(et == 0), stop=(et == 7), last=(et == 7), waits=w)
                mm_tok = em.tok("pe")
                tokE = em.v("tensor_copy", pstg[ft % 2][:], po[:],
                            waits=[mm_tok, last.get(("sc", ft % 2), ("gq", 0))])
                last[("ev", ft % 2)] = tokE
                t_sc = em.op("gpsimd", "indirect_dma_start",
                             shwr, IndirectOffsetOnAxis(
                                 fidx_sb[:, ft * 8 + tc:ft * 8 + tc + 1], 0),
                             pstg[ft % 2][:].bitcast(I32), None,
                             waits=[tokE, consts[0]], sem="gq", inc=16,
                             bounds_check=2 * 16384 - 1, oob_is_err=False)
                nscat += 1
                last[("sc", ft % 2)] = t_sc
            last[("yu", tc % 2)] = em.tok("pe")
        dscat_done = [em.tok("gq")]
        psD2.close()
        sD.close()
        em.phase()

        tokF = em.v("memset", flag_sb[:], MAGIC)
        em.op("gpsimd", "indirect_dma_start",
              shflag, IndirectOffsetOnAxis(flidx_sb[:], 0), flag_sb[:], None,
              waits=dscat_done + [tokF, consts[0]], sem="gq", inc=16,
              bounds_check=3, oob_is_err=False)
        if not dev_stub:
            sems = em.sems

            def poll_play(eng):
                cnt = eng.alloc_register("pollcnt")
                nd = eng.alloc_register("pollnd")
                eng.reg_mov(cnt, 0)
                eng.reg_mov(nd, 1)
                with eng.While(nd):
                    eng.reg_add(cnt, cnt, 16)
                    eng.indirect_dma_start(
                        pfl_sb[:], None, shflag,
                        IndirectOffsetOnAxis(plidx_sb[:], 0),
                        bounds_check=3, oob_is_err=False).then_inc(sems["pq"], 16)
                    eng.wait_ge(sems["pq"], cnt)
                    v = eng.value_load(pfl_sb[0:1, 0:1])
                    eng.reg_alu(nd, v, MAGIC, ALU.not_equal)
            em.raw("gpsimd", poll_play)

        sE = ExitStack()
        gm = [sE.enter_context(nc.sbuf_tensor(f"gm{i}", [128, 512], I32))
              for i in range(2)]
        gp_ = [sE.enter_context(nc.sbuf_tensor(f"gp{i}", [128, 512], I32))
               for i in range(2)]
        cmb_sb = sE.enter_context(nc.sbuf_tensor("cmb_sb", [128, 512], F32))
        attn_n = [sE.enter_context(nc.sbuf_tensor(f"attn_n{i}", [128, E], F32))
                  for i in range(4)]
        xoch = sE.enter_context(nc.sbuf_tensor("xoch", [128, E], F32))
        xmch = sE.enter_context(nc.sbuf_tensor("xmch", [128, E], F32))
        sqv2 = sE.enter_context(nc.sbuf_tensor("sqv2", [128, E], F32))
        h2c = sE.enter_context(nc.sbuf_tensor("h2c", [128, E], F32))
        h2cb = sE.enter_context(nc.sbuf_tensor("h2cb", [128, E], BF16))
        trsb = [sE.enter_context(nc.sbuf_tensor(f"trsb{i}", [128, 128], BF16))
                for i in range(2)]
        prob_sb = sE.enter_context(nc.sbuf_tensor("prob_sb", [128, 8], F32))
        nmax_sb = sE.enter_context(nc.sbuf_tensor("nmax_sb", [128, 1], F32))
        dsum_sb = sE.enter_context(nc.sbuf_tensor("dsum_sb", [128, 1], F32))
        eq8_sb = sE.enter_context(nc.sbuf_tensor("eq8_sb", [128, 8], F32))
        t8_sb = sE.enter_context(nc.sbuf_tensor("t8_sb", [128, 8], F32))
        psE = ExitStack()
        ptr = [psE.enter_context(nc.psum_tensor(f"ptr{i}", [128, 128], F32))
               for i in range(2)]
        ptrb = [psE.enter_context(nc.psum_tensor(f"ptrb{i}", [128, 128], BF16))
                for i in range(2)]
        psG = psE.enter_context(nc.psum_tensor("psG", [128, 8], F32))

        last = {}
        for i in range(4):
            for ft in range(16):
                col = ft * 4 + i
                t_gm = em.gather(gm[ft % 2][:], shrd,
                                 midx_sb[:, col:col + 1], 2 * 16384 - 1,
                                 waits=(em.phase_toks() if (i == 0 and ft == 0)
                                        else []) +
                                 [last.get(("gmu", ft % 2), ("dve", 0))])
                t_gp = em.gather(gp_[ft % 2][:], shrd,
                                 pidx_sb[:, col:col + 1], 2 * 16384 - 1,
                                 waits=[last.get(("gpu", ft % 2), ("dve", 0))])
                tokA = em.v("tensor_tensor", cmb_sb[:], gm[ft % 2][:].bitcast(F32),
                            gp_[ft % 2][:].bitcast(F32), ALU.add,
                            waits=[t_gm, t_gp, last.get("cmb", ("pe", 0))])
                last[("gmu", ft % 2)] = tokA
                last[("gpu", ft % 2)] = tokA
                tokA2 = em.v("tensor_scalar", cmb_sb[:], cmb_sb[:],
                             bo_sb[:, ft:ft + 1], None, ALU.add, waits=[t_bo])
                for q4 in range(4):
                    tokT = em.t("transpose", ptr[q4 % 2][:],
                                cmb_sb[:, q4 * 128:(q4 + 1) * 128], idnf_sb[:],
                                waits=[tokA2, t_idf,
                                       last.get(("trc", q4 % 2), ("dve", 0))])
                    tokTc = em.v("tensor_copy", attn_n[q4][:, ft * 128:(ft + 1) * 128],
                                 ptr[q4 % 2][:], waits=[tokT])
                    last[("trc", q4 % 2)] = tokTc
                last["cmb"] = em.tok("pe")
            attw = [em.tok("dve")]
            for q4 in range(4):
                tc128 = i * 4 + q4
                t_xo = em.dma("sync", xoch[:],
                              xown[tc128 * 128:(tc128 + 1) * 128, :],
                              waits=[last.get("xo", ("dve", 0))])
                tokXM = em.v("tensor_tensor", xmch[:], attn_n[q4][:], xoch[:],
                             ALU.add, waits=attw + [t_xo,
                                                    last.get("xm", ("aq", 0)),
                                                    last.get("xmq", ("act", 0))])
                last["xo"] = tokXM
                t_xm = em.dma("scalar", xmid_o[tc128 * 128:(tc128 + 1) * 128, :],
                              xmch[:], waits=[tokXM])
                last["xm"] = t_xm
                em.v("reduce_sum", red[:], xmch[:], axis=AX.X)
                tokQ = em.s("activation", sqv2[:], xmch[:], AF.Square,
                            waits=[tokXM])
                last["xmq"] = tokQ
                em.v("reduce_sum", red2[:], sqv2[:], axis=AX.X, waits=[tokQ])
                em.v("tensor_scalar", mmean[:], red[:], 1.0 / E, None, ALU.mult)
                em.v("tensor_scalar", red2[:], red2[:], 1.0 / E, None, ALU.mult)
                em.v("tensor_tensor", red[:], mmean[:], mmean[:], ALU.mult)
                tokV2 = em.v("tensor_tensor", red2[:], red2[:], red[:],
                             ALU.subtract)
                tokSD = em.s("activation", rstd[:], red2[:], AF.Sqrt,
                             bias=eps_sb[:, 0:1], waits=[tokV2, tokEPS])
                tokRI = em.v("reciprocal", rstd[:], rstd[:], waits=[tokSD])
                tokH = em.v("tensor_scalar", h2c[:], xmch[:], mmean[:], rstd[:],
                            ALU.subtract, ALU.mult,
                            waits=[tokRI, last.get("h2r", ("pe", 0))])
                t_h2 = em.dma("scalar", h2o_o[tc128 * 128:(tc128 + 1) * 128, :],
                              h2c[:], waits=[tokH])
                t_h2b = em.dma("scalar", h2f[tc128 * 128:(tc128 + 1) * 128, :],
                               h2c[:], waits=[tokH])
                last["xm"] = t_h2b
                tokHB = em.v("tensor_copy", h2cb[:], h2c[:],
                             waits=[tokH, last.get("h2br", ("pe", 0))])
                for et in range(16):
                    tokT = em.t("transpose", ptrb[et % 2][:],
                                h2cb[:, et * 128:(et + 1) * 128],
                                idnb_sb[:],
                                waits=[tokHB, t_idb,
                                       last.get(("tru", et % 2), ("dve", 0))])
                    tokTc = em.v("tensor_copy", trsb[et % 2][:], ptrb[et % 2][:],
                                 waits=[tokT])
                    last[("tru", et % 2)] = tokTc
                    em.t("matmul", psG[:], trsb[et % 2][:],
                         wg_sb[:, et * 8:(et + 1) * 8],
                         start=(et == 0), stop=(et == 15), last=(et == 15),
                         waits=[tokTc, t_wg] +
                         ([last.get("psg", ("dve", 0)),
                           last.get("psga", ("act", 0))] if et == 0 else []))
                last["h2r"] = em.tok("pe")
                last["h2br"] = em.tok("pe")
                pg_tok = em.tok("pe")
                tokNM = em.v("reduce_max", nmax_sb[:], psG[:], axis=AX.X,
                             negate=True, waits=[pg_tok])
                tokPR = em.s("activation", prob_sb[:], psG[:], AF.Exp,
                             bias=nmax_sb[:], accum_out=dsum_sb[:],
                             waits=[pg_tok, tokNM])
                tokGT = em.v("reciprocal", mgate_sb[:, tc128:tc128 + 1],
                             dsum_sb[:], waits=[tokPR])
                tokEQ = em.v("tensor_scalar", eq8_sb[:], psG[:], nmax_sb[:], 0.0,
                             ALU.add, ALU.is_ge, waits=[tokNM, pg_tok])
                last["psg"] = tokEQ
                last["psga"] = tokPR
                tokT8 = em.v("tensor_tensor", t8_sb[:], eq8_sb[:], iotam_sb[:],
                             ALU.mult, waits=[t_iom])
                tokT9 = em.v("tensor_scalar", t8_sb[:], t8_sb[:], 1000.0, None,
                             ALU.add)
                em.v("tensor_reduce", meidx_sb[:, tc128:tc128 + 1], t8_sb[:],
                     axis=AX.X, op=ALU.min)
        gate_done = [em.tok("dve"), em.tok("act")]
        psE.close()
        sE.close()
        em.phase()

        sF = ExitStack()
        OH2 = sF.enter_context(nc.sbuf_tensor("OH2", [128, 128], F32))
        cumS = sF.enter_context(nc.sbuf_tensor("cumS", [128, 128], F32))
        totrow = sF.enter_context(nc.sbuf_tensor("totrow", [1, 128], F32))
        totP = sF.enter_context(nc.sbuf_tensor("totP", [16, 8], F32))
        gpre = sF.enter_context(nc.sbuf_tensor("gpre", [16, 8], F32))
        cnt1 = sF.enter_context(nc.sbuf_tensor("cnt1", [1, 8], F32))
        gprow = sF.enter_context(nc.sbuf_tensor("gprow", [1, 128], F32))
        gpb = sF.enter_context(nc.sbuf_tensor("gpb", [128, 128], F32))
        posP = sF.enter_context(nc.sbuf_tensor("posP", [128, 128], F32))
        selp = sF.enter_context(nc.sbuf_tensor("selp", [128, 128], F32))
        scf = sF.enter_context(nc.sbuf_tensor("scf", [128, 16], F32))
        scov = sF.enter_context(nc.sbuf_tensor("scov", [128, 16], F32))
        sci = sF.enter_context(nc.sbuf_tensor("sci", [128, 16], I32))
        psF = ExitStack()
        pC = psF.enter_context(nc.psum_tensor("pC", [128, 128], F32))
        pT = psF.enter_context(nc.psum_tensor("pT", [1, 128], F32))
        pG16 = psF.enter_context(nc.psum_tensor("pG16", [16, 8], F32))
        pC8 = psF.enter_context(nc.psum_tensor("pC8", [1, 8], F32))
        pB2 = psF.enter_context(nc.psum_tensor("pB2", [128, 128], F32))

        for g in range(16):
            em.v("tensor_scalar", OH2[:, g * 8:(g + 1) * 8], iota8_sb[:],
                 meidx_sb[:, g:g + 1], None, ALU.is_equal,
                 waits=(em.phase_toks() if g == 0 else []) + [t_io8])
        tokOH = em.tok("dve")
        tokCU = em.t("matmul", pC[:], ut_sb[:], OH2[:], start=True, stop=True,
                     waits=[tokOH, t_ut])
        tokCS = em.v("tensor_copy", cumS[:], pC[:], waits=[tokCU])
        tokTT = em.t("matmul", pT[:], e127_sb[:], cumS[:], start=True, stop=True,
                     waits=[tokCS, t_e127])
        tokTR = em.v("tensor_copy", totrow[:], pT[:], waits=[tokTT])
        t_b1 = em.dma("scalar", bnc, totrow[:], waits=[tokTR])
        t_b2 = em.dma("sync", totP[:], bnc.rearrange("o (g e) -> (o g) e", g=16),
                      waits=[t_b1])
        tokG16 = em.t("matmul", pG16[:], stri16_sb[:], totP[:], start=True,
                      stop=True, waits=[t_b2, t_s16])
        tokC8 = em.t("matmul", pC8[:], ones16_sb[:], totP[:], start=True,
                     stop=True, waits=[t_o16])
        tokGP = em.v("tensor_copy", gpre[:], pG16[:], waits=[tokG16])
        tokCN = em.v("tensor_copy", cnt1[:], pC8[:], waits=[tokC8])
        t_cn = em.dma("scalar", cnt_o, cnt1[:], waits=[tokCN])
        t_b3 = em.dma("scalar", bnc2, gpre[:], waits=[tokGP])
        t_b4 = em.dma("sync", gprow[:],
                      bnc2.rearrange("(o g) e -> o (g e)", o=1), waits=[t_b3])
        tokB2 = em.t("matmul", pB2[:], o1x_sb[:], gprow[:], start=True, stop=True,
                     waits=[t_b4, t_o1x])
        tokPP = em.v("tensor_tensor", posP[:], cumS[:], pB2[:], ALU.add,
                     waits=[tokB2])
        tokSel = em.v("tensor_tensor", selp[:], OH2[:], posP[:], ALU.mult)
        tokRed = em.v("tensor_reduce",
                      mlpos_sb[:].rearrange("p (g e) -> p g e", g=1),
                      selp[:].rearrange("p (g e) -> p g e", g=16),
                      axis=AX.X, op=ALU.add, waits=[tokSel])
        tokLP = em.v("tensor_scalar", mlpos_sb[:], mlpos_sb[:], -1.0, None,
                     ALU.add)
        em.v("tensor_scalar", scf[:], meidx_sb[:], float(LCAP), None, ALU.mult,
             waits=[tokLP])
        em.v("tensor_tensor", scf[:], scf[:], mlpos_sb[:], ALU.add)
        em.v("tensor_scalar", scov[:], mlpos_sb[:], float(LCAP) - 0.5, None,
             ALU.is_ge)
        em.v("tensor_scalar", scov[:], scov[:], 65536.0, None, ALU.mult)
        em.v("tensor_tensor", scf[:], scf[:], scov[:], ALU.add)
        tokSCI = em.v("tensor_copy", sci[:], scf[:])
        zb = sF.enter_context(nc.sbuf_tensor("zb_l1", [128, E], F32))
        h2t = [sF.enter_context(nc.sbuf_tensor(f"h2t{i}", [128, E], F32))
               for i in range(2)]
        tokZ = em.dma("sync", zb[:], zrows)
        for zi in range(NE * LCAP // 128):
            em.dma("scalar", h2s_o[zi * 128:(zi + 1) * 128, :], zb[:],
                   waits=[tokZ])
        zdone = em.tok("aq")
        scw = {}
        for g in range(16):
            t_l = em.dma("sync", h2t[g % 2][:], h2f[g * 128:(g + 1) * 128, :],
                         waits=[scw.get(g % 2, ("gq", 0))])
            em.op("gpsimd", "indirect_dma_start",
                  h2s_o, IndirectOffsetOnAxis(sci[:, g:g + 1], 0),
                  h2t[g % 2][:], None,
                  waits=[tokSCI, zdone, t_l], sem="gq", inc=16,
                  bounds_check=NE * LCAP - 1, oob_is_err=False)
            scw[g % 2] = em.tok("gq")
        em.dma("scalar", meidx_o, meidx_sb[:], waits=[em.tok("dve")])
        em.dma("scalar", mgate_o, mgate_sb[:])
        em.dma("scalar", mlpos_o, mlpos_sb[:])
        psF.close()
        sF.close()

        em.finish("sync")
        em.playback(block)
    return nc



def l1_in_maps(inp):
    x = np.asarray(inp["L0_attn_x"]).reshape(B, T, E)
    WqT = np.ascontiguousarray(np.asarray(inp["L0_attn_Wq"]).T)
    WkT = np.ascontiguousarray(np.asarray(inp["L0_attn_Wk"]).T)
    WvT = np.ascontiguousarray(np.asarray(inp["L0_attn_Wv"]).T)
    WoT = np.ascontiguousarray(np.asarray(inp["L0_attn_Wo"]).T)
    bq = np.asarray(inp["L0_attn_bq"]); bk = np.asarray(inp["L0_attn_bk"])
    bv = np.asarray(inp["L0_attn_bv"]); bo = np.asarray(inp["L0_attn_bo"])
    wg = np.ascontiguousarray(np.asarray(inp["L0_ffn_Wg"]))

    j = np.arange(64, dtype=np.float64)
    thetas = 10000.0 ** (-2.0 * j / 128.0)
    m = np.arange(1, T + 1, dtype=np.float64)
    ang = m[None, :] * thetas[:, None]
    cosd = np.concatenate([np.cos(ang), np.cos(ang)], 0).astype(np.float32)
    sind = np.concatenate([np.sin(ang), np.sin(ang)], 0).astype(np.float32)

    dmask = np.zeros((128, 2048), np.float32)
    ki = np.arange(128)[:, None]
    qi = np.arange(512)[None, :]
    for v in range(4):
        valid = (qi >= ki + 128 * v)
        dmask[:, v * 512:(v + 1) * 512] = np.where(valid, 0.0, -30000.0)

    ut = np.zeros((128, 128), np.float32)
    for a in range(128):
        ut[a, a:] = 1.0
    stri16 = np.zeros((16, 16), np.float32)
    for a in range(16):
        stri16[a, a + 1:] = 1.0
    e127 = np.zeros((128, 1), np.float32); e127[127, 0] = 1.0
    iota8 = np.tile(np.arange(8, dtype=np.float32)[None, :], (128, 1))

    maps = []
    for c in range(NC):
        b, hh = c // 2, c % 2
        par = hh
        p = np.arange(128)
        fidx = np.zeros((128, 128), np.int32)
        for ft in range(16):
            for tcb in range(8):
                fidx[:, ft * 8 + tcb] = par * 16384 + (ft * 128 + p) * 8 + tcb
        pidx = np.zeros((128, 64), np.int32)
        midx = np.zeros((128, 64), np.int32)
        for ft in range(16):
            for i in range(4):
                tcb = hh * 4 + i
                pidx[:, ft * 4 + i] = (1 - par) * 16384 + (ft * 128 + p) * 8 + tcb
                midx[:, ft * 4 + i] = par * 16384 + (ft * 128 + p) * 8 + tcb
        maps.append({
            "xb": np.ascontiguousarray(x[b]),
            "xown": np.ascontiguousarray(x[b, hh * NT:(hh + 1) * NT]),
            "wqT": np.ascontiguousarray(WqT[:, hh * 1024:(hh + 1) * 1024]),
            "wkT": np.ascontiguousarray(WkT[:, hh * 1024:(hh + 1) * 1024]),
            "wvT": np.ascontiguousarray(WvT[:, hh * 1024:(hh + 1) * 1024]),
            "woT": np.ascontiguousarray(WoT[hh * 1024:(hh + 1) * 1024, :]),
            "bq": np.ascontiguousarray(
                bq[hh * 1024:(hh + 1) * 1024].reshape(8, 128).T),
            "bk": np.ascontiguousarray(
                bk[hh * 1024:(hh + 1) * 1024].reshape(8, 128).T),
            "bvrow": np.ascontiguousarray(bv[hh * 1024:(hh + 1) * 1024][None, :]),
            "bocol": np.ascontiguousarray(bo.reshape(16, 128).T),
            "wg": wg,
            "cosd": cosd, "sind": sind,
            "dmaskd": dmask,
            "ones128b": np.ones((128, 1), BF),
            "ones1x128": np.ones((1, 128), np.float32),
            "iotam": iota8 - 1000.0,
            "iota8b": iota8,
            "ut128": ut,
            "stri16": stri16,
            "ones16": np.ones((16, 1), np.float32),
            "e127": e127,
            "idnf": np.eye(128, dtype=np.float32),
            "idnb": np.eye(128, dtype=BF),
            "epsin": np.full((128, 1), 1.0e-5, np.float32),
            "sclin": np.full((128, 1), INVSQ, np.float32),
            "zrows": np.zeros((128, E), np.float32),
            "fidxd": fidx, "pidxd": pidx, "midxd": midx,
            "plidxd": np.array([[(1 - par) * 2], [(1 - par) * 2 + 1]], np.int32),
            "flidxd": np.array([[par * 2], [par * 2 + 1]], np.int32),
        })
    return maps



def kernel(L0_attn_x, L0_attn_Wq, L0_attn_bq, L0_attn_Wk, L0_attn_bk,
           L0_attn_Wv, L0_attn_bv, L0_attn_Wo, L0_attn_bo, L0_ffn_Wg,
           L0_ffn_W1, L0_ffn_W3, L0_ffn_W2, L0_ffn_sw1, L0_ffn_sw3,
           L0_ffn_sw2):
    inp = dict(
        L0_attn_x=np.asarray(L0_attn_x), L0_attn_Wq=np.asarray(L0_attn_Wq),
        L0_attn_bq=np.asarray(L0_attn_bq), L0_attn_Wk=np.asarray(L0_attn_Wk),
        L0_attn_bk=np.asarray(L0_attn_bk), L0_attn_Wv=np.asarray(L0_attn_Wv),
        L0_attn_bv=np.asarray(L0_attn_bv), L0_attn_Wo=np.asarray(L0_attn_Wo),
        L0_attn_bo=np.asarray(L0_attn_bo), L0_ffn_Wg=np.asarray(L0_ffn_Wg),
        L0_ffn_W1=np.asarray(L0_ffn_W1), L0_ffn_W3=np.asarray(L0_ffn_W3),
        L0_ffn_W2=np.asarray(L0_ffn_W2), L0_ffn_sw1=np.asarray(L0_ffn_sw1),
        L0_ffn_sw3=np.asarray(L0_ffn_sw3), L0_ffn_sw2=np.asarray(L0_ffn_sw2),
    )
    cores = list(range(NC))

    maps1 = l1_in_maps(inp)
    nc1 = build_l1()
    r1 = run_bass_kernel_spmd(nc1, maps1, core_ids=cores).results

    counts_all = np.concatenate([np.asarray(r1[c]["cnt"]) for c in cores], axis=0)
    h2sorted_all = [np.asarray(r1[c]["h2s"]) for c in cores]
    maps2 = l2_in_maps(h2sorted_all, counts_all,
                       inp["L0_ffn_W1"], inp["L0_ffn_W3"], inp["L0_ffn_W2"])
    nc2 = build_l2()
    r2 = run_bass_kernel_spmd(nc2, maps2, core_ids=cores).results

    oe_blocks = [np.asarray(r2[e]["oe"]) for e in cores]
    h2own_l = [np.asarray(r1[c]["h2o"]) for c in cores]
    xmid_l = [np.asarray(r1[c]["xmid"]) for c in cores]
    meta_l = [(np.asarray(r1[c]["m_eidx"]), np.asarray(r1[c]["m_gate"]),
               np.asarray(r1[c]["m_lpos"])) for c in cores]
    maps3 = l3_in_maps(oe_blocks, h2own_l, xmid_l, meta_l, counts_all,
                       inp["L0_ffn_sw1"], inp["L0_ffn_sw3"], inp["L0_ffn_sw2"])
    nc3 = build_l3()
    r3 = run_bass_kernel_spmd(nc3, maps3, core_ids=cores).results

    out = np.concatenate([np.asarray(r3[c]["out"]) for c in cores], axis=0)
    return np.ascontiguousarray(out.reshape(B, T, E).astype(np.float32))


# version 06
# speedup vs baseline: 1.2104x; 1.2104x over previous
import math
from contextlib import ExitStack

import numpy as np
import ml_dtypes
import concourse.bass as bass
import concourse.mybir as mybir
from concourse.bass_core_rust import IndirectOffsetOnAxis
from concourse.bass_utils import run_bass_kernel_spmd

F32 = mybir.dt.float32
BF16 = mybir.dt.bfloat16
I32 = mybir.dt.int32
AF = mybir.ActivationFunctionType
ALU = mybir.AluOpType
AX = mybir.AxisListType

B, T, E, H, D = 4, 4096, 2048, 16, 128
NE, HF, CAP = 8, 2816, 2048
LCAP = 512
NC = 8
NT = 2048
BF = ml_dtypes.bfloat16
INVSQ = 1.0 / math.sqrt(128.0)



class Em:
    ENGINES = ("sync", "scalar", "vector", "tensor", "gpsimd")

    def __init__(self, nc):
        self.nc = nc
        self.q = {e: [] for e in self.ENGINES}
        self.cnt = {}
        self.sems = {}
        self.hiwater = {e: {} for e in self.ENGINES}
        self._bregs = {}
        self.watermark = {}
        self.floor = []

    def sem(self, stack, *names):
        for n in names:
            self.sems[n] = stack.enter_context(self.nc.semaphore(n))
            self.cnt[n] = 0

    DMA_SEMS = ("sq", "aq", "gq", "wq")

    def phase(self):
        self.floor = self.phase_toks()

    def _waits(self, eng_name, waits):
        out = []
        hw = self.hiwater[eng_name]
        for s, v in list(waits) + self.floor:
            if s in self.DMA_SEMS:
                v = self.cnt[s]
            if v > 0 and hw.get(s, 0) < v:
                hw[s] = v
                out.append((s, v))
            if v > self.watermark.get(s, 0):
                self.watermark[s] = v
        return out

    def op(self, eng_name, method, *args, waits=(), sem=None, inc=0, **kwargs):
        waits = list(waits)
        if inc and sem in self.DMA_SEMS:
            wm = self.watermark.get(sem, 0)
            if wm > 0:
                waits.append((sem, wm))
        wl = self._waits(eng_name, waits)
        tok = None
        if inc:
            self.cnt[sem] += inc
            tok = (sem, self.cnt[sem])
        sems = self.sems

        bregs = self._bregs

        def play(eng, wl=wl, method=method, args=args, kwargs=kwargs, inc=inc, sem=sem):
            for s, v in wl:
                eng.wait_ge(sems[s], v)
            if (method == "indirect_dma_start"
                    and isinstance(kwargs.get("bounds_check"), int)):
                val = kwargs["bounds_check"]
                reg = bregs.get(val)
                if reg is None:
                    reg = eng.alloc_register(f"bnd{val}")
                    eng.reg_mov(reg, val)
                    bregs[val] = reg
                kwargs = dict(kwargs, bounds_check=reg)
            try:
                inst = getattr(eng, method)(*args, **kwargs)
            except Exception as e:
                raise RuntimeError(f"emit failed: {method} arg1={args[1] if len(args)>1 else None} arg2={args[2] if len(args)>2 else None} kw={kwargs}") from e
            if inc:
                inst.then_inc(sems[sem], inc)
        self.q[eng_name].append(play)
        return tok

    def t(self, method, *a, waits=(), last=True, **k):
        return self.op("tensor", method, *a, waits=waits,
                       sem="pe" if last else None, inc=1 if last else 0, **k)

    def v(self, method, *a, waits=(), **k):
        w = list(waits) + [self.tok("dve")]
        return self.op("vector", method, *a, waits=w, sem="dve", inc=1, **k)

    def s(self, method, *a, waits=(), **k):
        w = list(waits) + [self.tok("act")]
        return self.op("scalar", method, *a, waits=w, sem="act", inc=1, **k)

    def g(self, method, *a, waits=(), **k):
        w = list(waits) + [self.tok("pool")]
        return self.op("gpsimd", method, *a, waits=w, sem="pool", inc=1, **k)

    def dma(self, queue, out, in_, waits=(), sem=None, **k):
        sem = sem or {"sync": "sq", "scalar": "aq", "gpsimd": "gq"}[queue]
        return self.op(queue, "dma_start", waits=waits, sem=sem, inc=16,
                       out=out, in_=in_, **k)

    def dmat(self, queue, out, in_, waits=()):
        return self.dma(queue, out, in_, waits=waits, transpose=True)

    def gather(self, out, in_, idx_ap, bounds, waits=()):
        return self.op("gpsimd", "indirect_dma_start",
                       out, None, in_, IndirectOffsetOnAxis(idx_ap, 0),
                       waits=waits, sem="gq", inc=16,
                       bounds_check=bounds, oob_is_err=False)

    def scatter(self, out, idx_ap, in_, bounds, waits=()):
        return self.op("gpsimd", "indirect_dma_start",
                       out, IndirectOffsetOnAxis(idx_ap, 0), in_, None,
                       waits=waits, sem="gq", inc=16,
                       bounds_check=bounds, oob_is_err=False)

    def raw(self, eng_name, fn):
        self.q[eng_name].append(fn)

    def tok(self, sem):
        return (sem, self.cnt[sem])

    def phase_toks(self):
        return [(s, c) for s, c in self.cnt.items() if c > 0 and s != "pq"]

    def finish(self, eng_name="sync"):
        waits = [(s, c) for s, c in self.cnt.items() if c > 0]
        sems = self.sems

        def play(eng, waits=waits):
            for s, v in waits:
                eng.wait_ge(sems[s], v)
        self.q[eng_name].append(play)

    def playback(self, block):
        for name in self.ENGINES:
            ops = self.q[name]
            if not ops:
                continue

            def body(eng, ops=ops):
                for f in ops:
                    f(eng)
            getattr(block, name)(body)



HT_GROUPS = ((0, 8), (8, 15), (15, 22))


def emit_ffn(em, nc, st, tag, hT, p1, p3, po, xn_dram, w1_dram, w3_dram,
             w2_dram, consume, x_ready, stage2_hook=None, w13_init=()):
    from contextlib import ExitStack
    s1 = ExitStack()
    w1sb = s1.enter_context(nc.sbuf_tensor(tag + "w1sb", [128, 16 * 1024], BF16))
    w3sb = s1.enter_context(nc.sbuf_tensor(tag + "w3sb", [128, 16 * 1024], BF16))
    xsb = s1.enter_context(nc.sbuf_tensor(tag + "xsb", [128, 16 * 1024], BF16))
    silu_sb = s1.enter_context(nc.sbuf_tensor(tag + "silu", [128, 1024], F32))
    xcons = []
    w13cons = list(w13_init)
    p1_rd = None
    p3_rd = None
    for (h0, h1) in HT_GROUPS:
        nht = h1 - h0
        t_w1 = em.dma("gpsimd",
                      w1sb[:].rearrange("p (dt k) -> p dt k", dt=16)[:, :, 0:nht * 128],
                      w1_dram[:, h0 * 128:h1 * 128].rearrange("(dt p) h -> p dt h", p=128),
                      waits=w13cons, sem="wq")
        t_w3 = em.dma("gpsimd",
                      w3sb[:].rearrange("p (dt k) -> p dt k", dt=16)[:, :, 0:nht * 128],
                      w3_dram[:, h0 * 128:h1 * 128].rearrange("(dt p) h -> p dt h", p=128),
                      sem="wq")
        for sh in range(2):
            xt_toks = []
            for dt in range(16):
                xt_toks.append(em.dmat(
                    "sync", xsb[:, dt * 1024:(dt + 1) * 1024],
                    xn_dram[sh * 1024:(sh + 1) * 1024, dt * 128:(dt + 1) * 128],
                    waits=(x_ready + xcons) if dt == 0 else ()))
            xcons = []
            for ht in range(h0, h1):
                hr = ht - h0
                for (psum, wsb, t_w, rd) in ((p1, w1sb, t_w1, p1_rd),
                                             (p3, w3sb, t_w3, p3_rd)):
                    first_w = [t_w, xt_toks[-1]] + ([rd] if rd else [])
                    for dt in range(16):
                        for nk in range(2):
                            em.t("matmul",
                                 psum[:, nk * 512:(nk + 1) * 512],
                                 wsb[:, dt * 1024 + hr * 128:dt * 1024 + hr * 128 + 128],
                                 xsb[:, dt * 1024 + nk * 512:dt * 1024 + (nk + 1) * 512],
                                 start=(dt == 0), stop=(dt == 15),
                                 last=(dt == 15 and nk == 1),
                                 waits=first_w if (dt == 0 and nk == 0) else ())
                    first_w = []
                tok_p = em.tok("pe")
                tok_silu = em.s("activation", silu_sb[:], p1[:], AF.Silu,
                                waits=[tok_p])
                tok_h = em.v("tensor_tensor",
                             hT[:, ht * 2048 + sh * 1024: ht * 2048 + (sh + 1) * 1024],
                             p3[:], silu_sb[:], ALU.mult,
                             waits=[tok_silu])
                p1_rd, p3_rd = tok_silu, tok_h
            xcons = [em.tok("pe")]
        w13cons = [em.tok("pe")]
    hT_done = [em.tok("act"), em.tok("dve")]
    stage1_pe = em.tok("pe")
    s1.close()
    w2sb = st.enter_context(nc.sbuf_tensor(tag + "w2sb", [128, 22 * 2048], BF16))
    if stage2_hook is not None:
        stage2_hook()
    t_w2 = em.dma("gpsimd",
                  w2sb[:].rearrange("p (ht d) -> p ht d", ht=22),
                  w2_dram.rearrange("(ht p) d -> p ht d", p=128),
                  waits=[stage1_pe, em.tok("sq"), em.tok("aq"), em.tok("act")],
                  sem="wq")

    cons_toks = []
    for ct in range(16):
        first = True
        for ht in range(22):
            for nk in range(4):
                w = []
                if first:
                    w = hT_done + [t_w2] + cons_toks
                    first = False
                em.t("matmul", po[:, nk * 512:(nk + 1) * 512],
                     hT[:, ht * 2048 + ct * 128: ht * 2048 + ct * 128 + 128],
                     w2sb[:, ht * 2048 + nk * 512: ht * 2048 + (nk + 1) * 512],
                     start=(ht == 0), stop=(ht == 21),
                     last=(ht == 21 and nk == 3), waits=w)
        cons_toks = consume(ct, po, em.tok("pe"))



def build_l2():
    nc = bass.Bass("TRN2", target_bir_lowering=False, debug=False)
    h2s = nc.dram_tensor("h2s", [NE * LCAP, E], F32, kind="ExternalInput").ap()
    cnt_in = nc.dram_tensor("cnt", [8, 1], F32, kind="ExternalInput").ap()
    piota = nc.dram_tensor("piota", [128, 16], F32, kind="ExternalInput").ap()
    eoff = nc.dram_tensor("eoff", [128, 1], F32, kind="ExternalInput").ap()
    stri9 = nc.dram_tensor("stri9", [8, 9], F32, kind="ExternalInput").ap()
    ones1x128 = nc.dram_tensor("ones1x128", [1, 128], F32, kind="ExternalInput").ap()
    zrows = nc.dram_tensor("zrows", [128, E], F32, kind="ExternalInput").ap()
    w1 = nc.dram_tensor("w1", [E, HF], F32, kind="ExternalInput").ap()
    w3 = nc.dram_tensor("w3", [E, HF], F32, kind="ExternalInput").ap()
    w2 = nc.dram_tensor("w2", [HF, E], F32, kind="ExternalInput").ap()
    oe_out = nc.dram_tensor("oe", [CAP, E], F32, kind="ExternalOutput").ap()
    xd_dram = nc.dram_tensor("xd", [CAP, E], BF16, kind="Internal").ap()

    with ExitStack() as st:
        def sb(name, shape, dt=F32):
            return st.enter_context(nc.sbuf_tensor(name, shape, dt))

        def ps(name, shape, dt=F32):
            return st.enter_context(nc.psum_tensor(name, shape, dt))

        em = Em(nc)
        em.sem(st, "sq", "aq", "gq", "wq", "pe", "act", "dve", "pool")

        cnt_sb = sb("cnt_sb", [8, 1])
        stri_sb = sb("stri_sb", [8, 9])
        ones_sb = sb("ones_sb", [1, 128])
        piota_sb = sb("piota_sb", [128, 16])
        eoff_sb = sb("eoff_sb", [128, 1])
        cum9_sb = sb("cum9_sb", [1, 9])
        cumb_sb = sb("cumb_sb", [128, 9])
        accS = sb("accS", [128, 16])
        cumsel = sb("cumsel", [128, 16])
        ge = sb("ge", [128, 16])
        geN = sb("geN", [128, 16])
        tmp16 = sb("tmp16", [128, 16])
        rowf = sb("rowf", [128, 16])
        idx32 = sb("idx32", [128, 16], I32)
        hT = sb("hT", [128, 22 * 2048], BF16)
        oe_sb = sb("oe_sb", [128, E])
        oe_sb2 = sb("oe_sb2", [128, E])
        zscope = ExitStack()
        zrow = zscope.enter_context(nc.sbuf_tensor("zrow", [128, E], F32))
        zrow2 = zscope.enter_context(nc.sbuf_tensor("zrow2", [128, E], F32))

        from contextlib import ExitStack as _ES
        ps_scope = _ES()
        p_r = ps_scope.enter_context(nc.psum_tensor("p_r", [1, 9], F32))
        p_b = ps_scope.enter_context(nc.psum_tensor("p_b", [128, 9], F32))

        block = st.enter_context(nc.Block())

        t_cnt = em.dma("sync", cnt_sb[:], cnt_in)
        t_stri = em.dma("sync", stri_sb[:], stri9)
        t_ones = em.dma("sync", ones_sb[:], ones1x128)
        t_pio = em.dma("sync", piota_sb[:], piota)
        t_eof = em.dma("sync", eoff_sb[:], eoff)

        tok = em.t("matmul", p_r[:], cnt_sb[:], stri_sb[:], start=True, stop=True,
                   waits=[t_cnt, t_stri])
        tok_c9 = em.v("tensor_copy", cum9_sb[:], p_r[:], waits=[tok])
        tokb = em.t("matmul", p_b[:], ones_sb[:], cum9_sb[:], start=True, stop=True,
                    waits=[tok_c9, t_ones])
        tok_cb = em.v("tensor_copy", cumb_sb[:], p_b[:], waits=[tokb])

        em.v("tensor_scalar", accS[:], piota_sb[:], 0.0, None, ALU.mult,
             waits=[t_pio])
        em.v("tensor_scalar", cumsel[:], piota_sb[:], 0.0, None, ALU.mult)
        for s in range(1, 9):
            dst = geN if s == 8 else ge
            em.v("tensor_scalar", dst[:], piota_sb[:], cumb_sb[:, s:s + 1], None,
                 ALU.is_ge, waits=[tok_cb, t_pio])
            em.v("tensor_tensor", accS[:], accS[:], dst[:], ALU.add)
            em.v("tensor_scalar", tmp16[:], dst[:], cumb_sb[:, s:s + 1], None,
                 ALU.mult)
            em.v("tensor_tensor", cumsel[:], cumsel[:], tmp16[:], ALU.max)
        em.v("tensor_scalar", accS[:], accS[:], float(LCAP), None, ALU.mult)
        em.v("tensor_tensor", rowf[:], piota_sb[:], cumsel[:], ALU.subtract)
        em.v("tensor_tensor", rowf[:], rowf[:], accS[:], ALU.add)
        em.v("tensor_scalar", rowf[:], rowf[:], eoff_sb[:, 0:1], None, ALU.add,
             waits=[t_eof])
        em.v("tensor_scalar", tmp16[:], geN[:], 65536.0, None, ALU.mult)
        em.v("tensor_tensor", rowf[:], rowf[:], tmp16[:], ALU.add)
        tok_idx = em.v("tensor_copy", idx32[:], rowf[:])

        gts = []
        st_tok = None
        tz0 = em.dma("sync", zrow[:], zrows)
        tz1 = em.dma("sync", zrow2[:], zrows)
        for j in range(16):
            buf = zrow if j % 2 == 0 else zrow2
            w = [tok_idx, tz0, tz1]
            if st_tok is not None:
                w.append(st_tok)
            if j >= 2:
                tok_z = em.dma("sync", buf[:], zrows, waits=w)
                w = [tok_z]
            gt = em.gather(buf[:], h2s, idx32[:, j:j + 1], NE * LCAP - 1,
                           waits=w)
            st_tok = em.dma("gpsimd", xd_dram[j * 128:(j + 1) * 128, :], buf[:],
                            waits=[gt])
            gts.append(st_tok)
        x_ready = [gts[-1]]

        def consume(ct, po_ap, pe_tok):
            buf = oe_sb if ct % 2 == 0 else oe_sb2
            tok_e = em.v("tensor_copy", buf[:], po_ap[:], waits=[pe_tok])
            tok_o = em.dma("scalar", oe_out[ct * 128:(ct + 1) * 128, :], buf[:],
                           waits=[tok_e])
            return [tok_e, tok_o]

        ps_scope.close()
        zscope.close()
        p1 = ps("p1", [128, 1024])
        p3 = ps("p3", [128, 1024])
        po = ps("po", [128, 2048])
        emit_ffn(em, nc, st, "f_", hT, p1, p3, po,
                 xd_dram, w1, w3, w2, consume, x_ready, w13_init=x_ready)

        em.finish("sync")
        em.playback(block)
    return nc



def _bf(x):
    return np.ascontiguousarray(x.astype(BF) if x.dtype != BF else x)


def l2_in_maps(h2sorted_all, counts_all, w1, w3, w2):
    piota = (np.arange(128)[:, None] + 128 * np.arange(16)[None, :]).astype(np.float32)
    stri9 = np.zeros((8, 9), np.float32)
    for k in range(8):
        for s in range(9):
            stri9[k, s] = 1.0 if k < s else 0.0
    ones1x128 = np.ones((1, 128), np.float32)
    zrows = np.zeros((128, E), np.float32)
    maps = []
    for e in range(NC):
        h2s_e = np.ascontiguousarray(np.concatenate(
            [np.asarray(h2sorted_all[s][e * LCAP:(e + 1) * LCAP], np.float32)
             for s in range(NC)], axis=0))
        maps.append({
            "h2s": h2s_e,
            "zrows": zrows,
            "cnt": np.ascontiguousarray(counts_all[:, e:e + 1]).astype(np.float32),
            "piota": piota,
            "eoff": np.zeros((128, 1), np.float32),
            "stri9": stri9,
            "ones1x128": ones1x128,
            "w1": np.ascontiguousarray(w1[e]),
            "w3": np.ascontiguousarray(w3[e]),
            "w2": np.ascontiguousarray(w2[e]),
        })
    return maps



def build_l3():
    nc = bass.Bass("TRN2", target_bir_lowering=False, debug=False)
    oeall = nc.dram_tensor("oeall", [NE * CAP, E], F32, kind="ExternalInput").ap()
    h2own = nc.dram_tensor("h2own", [NT, E], F32, kind="ExternalInput").ap()
    h2b = nc.dram_tensor("h2b", [NT, E], BF16, kind="Internal").ap()
    xmid = nc.dram_tensor("xmid", [NT, E], F32, kind="ExternalInput").ap()
    m_eidx = nc.dram_tensor("m_eidx", [128, 16], F32, kind="ExternalInput").ap()
    m_gate = nc.dram_tensor("m_gate", [128, 16], F32, kind="ExternalInput").ap()
    m_lpos = nc.dram_tensor("m_lpos", [128, 16], F32, kind="ExternalInput").ap()
    cnts = nc.dram_tensor("cnts", [8, 8], F32, kind="ExternalInput").ap()
    rowsel = nc.dram_tensor("rowsel", [8, 1], F32, kind="ExternalInput").ap()
    stri8 = nc.dram_tensor("stri8", [8, 8], F32, kind="ExternalInput").ap()
    ones1x128 = nc.dram_tensor("ones1x128", [1, 128], F32, kind="ExternalInput").ap()
    sw1 = nc.dram_tensor("sw1", [E, HF], F32, kind="ExternalInput").ap()
    sw3 = nc.dram_tensor("sw3", [E, HF], F32, kind="ExternalInput").ap()
    sw2 = nc.dram_tensor("sw2", [HF, E], F32, kind="ExternalInput").ap()
    out = nc.dram_tensor("out", [NT, E], F32, kind="ExternalOutput").ap()

    with ExitStack() as st:
        def sb(name, shape, dt=F32):
            return st.enter_context(nc.sbuf_tensor(name, shape, dt))

        def ps(name, shape, dt=F32):
            return st.enter_context(nc.psum_tensor(name, shape, dt))

        em = Em(nc)
        em.sem(st, "sq", "aq", "gq", "wq", "pe", "act", "dve", "pool")

        cnts_sb = sb("cnts_sb", [8, 8])
        rsel_sb = sb("rsel_sb", [8, 1])
        stri_sb = sb("stri_sb", [8, 8])
        ones_sb = sb("ones_sb", [1, 128])
        bp_sb = sb("bp_sb", [8, 8])
        myrow_sb = sb("myrow_sb", [1, 8])
        bpb_sb = sb("bpb_sb", [128, 8])
        eidx_sb = sb("eidx_sb", [128, 16])
        gate_sb = sb("gate_sb", [128, 16])
        lpos_sb = sb("lpos_sb", [128, 16])
        acc_sb = sb("acc_sb", [128, 16])
        tmp_sb = sb("tmp_sb", [128, 16])
        keep_sb = sb("keep_sb", [128, 16])
        gk_sb = sb("gk_sb", [128, 16])
        flat_sb = sb("flat_sb", [128, 16])
        idx32 = sb("idx32", [128, 16], I32)
        combine = {}

        from contextlib import ExitStack as _ES
        ps_scope = _ES()
        p_r8 = ps_scope.enter_context(nc.psum_tensor("p_r8", [8, 8], F32))
        p_m = ps_scope.enter_context(nc.psum_tensor("p_m", [1, 8], F32))
        p_bb = ps_scope.enter_context(nc.psum_tensor("p_bb", [128, 8], F32))

        block = st.enter_context(nc.Block())

        t_c = em.dma("sync", cnts_sb[:], cnts)
        t_r = em.dma("sync", rsel_sb[:], rowsel)
        t_s = em.dma("sync", stri_sb[:], stri8)
        t_o = em.dma("sync", ones_sb[:], ones1x128)
        t_e = em.dma("sync", eidx_sb[:], m_eidx)
        t_g = em.dma("sync", gate_sb[:], m_gate)
        t_l = em.dma("sync", lpos_sb[:], m_lpos)

        tok = em.t("matmul", p_r8[:], stri_sb[:], cnts_sb[:], start=True, stop=True,
                   waits=[t_c, t_s])
        tok_bp = em.v("tensor_copy", bp_sb[:], p_r8[:], waits=[tok])
        tok = em.t("matmul", p_m[:], rsel_sb[:], bp_sb[:], start=True, stop=True,
                   waits=[tok_bp, t_r])
        tok_mr = em.v("tensor_copy", myrow_sb[:], p_m[:], waits=[tok])
        tok = em.t("matmul", p_bb[:], ones_sb[:], myrow_sb[:], start=True, stop=True,
                   waits=[tok_mr, t_o])
        tok_bb = em.v("tensor_copy", bpb_sb[:], p_bb[:], waits=[tok])

        em.v("tensor_scalar", acc_sb[:], eidx_sb[:], 0.0, None, ALU.mult,
             waits=[t_e])
        for e in range(8):
            em.v("tensor_scalar", tmp_sb[:], eidx_sb[:], float(e), None,
                 ALU.is_equal, waits=[t_e])
            em.v("tensor_scalar", tmp_sb[:], tmp_sb[:], bpb_sb[:, e:e + 1], None,
                 ALU.mult, waits=[tok_bb])
            em.v("tensor_tensor", acc_sb[:], acc_sb[:], tmp_sb[:], ALU.add)
        em.v("tensor_tensor", acc_sb[:], acc_sb[:], lpos_sb[:], ALU.add,
             waits=[t_l])
        em.v("tensor_scalar", keep_sb[:], acc_sb[:], 2047.5, None, ALU.is_le)
        em.v("tensor_scalar", flat_sb[:], eidx_sb[:], float(CAP), None, ALU.mult)
        em.v("tensor_tensor", flat_sb[:], flat_sb[:], acc_sb[:], ALU.add)
        em.v("tensor_scalar", flat_sb[:], flat_sb[:], float(NE * CAP - 1), None,
             ALU.min)
        em.v("tensor_tensor", gk_sb[:], gate_sb[:], keep_sb[:], ALU.mult,
             waits=[t_g])
        tok_idx = em.v("tensor_copy", idx32[:], flat_sb[:])

        ps_scope.close()
        p1 = ps("p1", [128, 1024])
        p3 = ps("p3", [128, 1024])
        po = ps("po", [128, 2048])

        pre = {}
        done_toks = {}

        def stage2_hook():
            combine["oeg"] = [sb("oeg0", [128, E]), sb("oeg1", [128, E])]
            combine["xms"] = [sb("xms0", [128, E], BF16), sb("xms1", [128, E], BF16)]

        def prefetch(ct, extra_waits):
            w = extra_waits + [em.tok("pe"), em.tok("act"), em.tok("dve")]
            pre[ct] = (
                em.gather(combine["oeg"][ct % 2][:], oeall, idx32[:, ct:ct + 1],
                          NE * CAP - 1, waits=[tok_idx] + w),
                em.dma("gpsimd", combine["xms"][ct % 2][:],
                       xmid[ct * 128:(ct + 1) * 128, :], waits=w),
            )

        def consume(ct, po_ap, pe_tok):
            if ct == 0:
                prefetch(0, [])
                prefetch(1, [])
            g_tok, x_tok = pre[ct]
            acc = combine["oeg"][ct % 2]
            t1 = em.v("tensor_scalar", acc[:], acc[:],
                      gk_sb[:, ct:ct + 1], None, ALU.mult, waits=[g_tok])
            t2 = em.v("tensor_tensor", acc[:], acc[:], po_ap[:], ALU.add,
                      waits=[pe_tok])
            t3 = em.v("tensor_tensor", acc[:], acc[:],
                      combine["xms"][ct % 2][:], ALU.add, waits=[x_tok])
            t4 = em.dma("scalar", out[ct * 128:(ct + 1) * 128, :],
                        acc[:], waits=[t3])
            done_toks[ct] = t4
            if ct + 2 < 16:
                prefetch(ct + 2, [t3, t4])
            return [t2]

        t_h2b = em.dma("gpsimd", h2b, h2own, waits=[])
        emit_ffn(em, nc, st, "s_", sb("hT", [128, 22 * 2048], BF16),
                 p1, p3, po, h2b, sw1, sw3, sw2, consume, [tok_idx, t_h2b],
                 stage2_hook=stage2_hook)

        em.finish("sync")
        em.playback(block)
    return nc


def l3_in_maps(oe_blocks, h2own_l, xmid_l, meta_l, counts_all, sw1, sw3, sw2):
    oeall = np.ascontiguousarray(np.concatenate(oe_blocks, axis=0))
    stri8 = np.zeros((8, 8), np.float32)
    for k in range(8):
        for s in range(8):
            stri8[k, s] = 1.0 if k < s else 0.0
    ones1x128 = np.ones((1, 128), np.float32)
    maps = []
    for c in range(NC):
        rowsel = np.zeros((8, 1), np.float32)
        rowsel[c, 0] = 1.0
        maps.append({
            "oeall": oeall,
            "h2own": h2own_l[c],
            "xmid": xmid_l[c],
            "m_eidx": meta_l[c][0], "m_gate": meta_l[c][1], "m_lpos": meta_l[c][2],
            "cnts": counts_all.astype(np.float32),
            "rowsel": rowsel,
            "stri8": stri8,
            "ones1x128": ones1x128,
            "sw1": np.ascontiguousarray(sw1),
            "sw3": np.ascontiguousarray(sw3),
            "sw2": np.ascontiguousarray(sw2),
        })
    return maps


MAGIC = 19450816


def build_l1(dev_stub=False, dev_outputs=False):
    nc = bass.Bass("TRN2", target_bir_lowering=False, debug=False,
                   detect_race_conditions=dev_stub)
    din = lambda n, s, dt=F32: nc.dram_tensor(n, s, dt, kind="ExternalInput").ap()
    dout = lambda n, s, dt=F32: nc.dram_tensor(n, s, dt, kind="ExternalOutput").ap()
    dint = lambda n, s, dt=F32: nc.dram_tensor(
        n, s, dt, kind="ExternalOutput" if dev_outputs else "Internal").ap()

    xb = din("xb", [T, E])
    xown = din("xown", [NT, E])
    wqT = din("wqT", [E, 1024]); wkT = din("wkT", [E, 1024]); wvT = din("wvT", [E, 1024])
    woT = din("woT", [E, E])
    bq = din("bq", [128, 8]); bk = din("bk", [128, 8])
    bvrow = din("bvrow", [1, 1024])
    bocol = din("bocol", [128, 16])
    wg = din("wg", [E, 8])
    cosd = din("cosd", [128, T]); sind = din("sind", [128, T])
    dmaskd = din("dmaskd", [128, 2048])
    ones128b = din("ones128b", [128, 1], BF16)
    ones1x128 = din("ones1x128", [1, 128])
    iotam = din("iotam", [128, 8])
    iota8b = din("iota8b", [128, 8])
    ut128 = din("ut128", [128, 128])
    stri16 = din("stri16", [16, 16])
    ones16 = din("ones16", [16, 1])
    e127 = din("e127", [128, 1])
    idnf = din("idnf", [128, 128])
    idnb = din("idnb", [128, 128], BF16)
    epsin = din("epsin", [128, 1])
    sclin = din("sclin", [128, 1])
    zrows = din("zrows", [128, E])
    scidxd = din("scidxd", [128, 64], I32)
    goidxd = din("goidxd", [128, 64], I32)
    plidxd = din("plidxd", [2, 1], I32)
    flidxd = din("flidxd", [2, 1], I32)

    xmid_o = dout("xmid", [NT, E])
    h2o_o = dout("h2o", [NT, E])
    h2s_o = dout("h2s", [NE * LCAP, E])
    cnt_o = dout("cnt", [1, 8])
    meidx_o = dout("m_eidx", [128, 16])
    mgate_o = dout("m_gate", [128, 16])
    mlpos_o = dout("m_lpos", [128, 16])

    lnN = dint("lnN", [T, E], BF16)
    h2f = nc.dram_tensor("h2f", [NT, E], F32, kind="Internal").ap()
    qTd = dint("qTd", [1024, T], BF16)
    kTd = dint("kTd", [1024, T], BF16)
    vNd = dint("vNd", [T, 1024], BF16)
    bnc = nc.dram_tensor("bnc", [1, 128], F32, kind="Internal").ap()
    bnc2 = nc.dram_tensor("bnc2", [16, 8], F32, kind="Internal").ap()
    if dev_stub:
        shrd = din("pp", [2 * 8192, 512], I32)
        shwr = nc.dram_tensor("shsc", [2 * 8192, 512], I32, kind="Internal").ap()
        shflag = nc.dram_tensor("shflag", [4, 64], I32, kind="Internal").ap()
    else:
        shrd = nc.dram_tensor("shpair", [2 * 8192, 512], I32, kind="Internal",
                              addr_space="Shared").ap()
        shwr = shrd
        shflag = nc.dram_tensor("shflag", [4, 64], I32, kind="Internal",
                                addr_space="Shared").ap()

    with ExitStack() as st:
        def sb(name, shape, dt=F32):
            return st.enter_context(nc.sbuf_tensor(name, shape, dt))

        em = Em(nc)
        em.sem(st, "sq", "aq", "gq", "wq", "pq", "pe", "act", "dve", "pool")

        bq_sb = sb("bq_sb", [128, 8]); bk_sb = sb("bk_sb", [128, 8])
        bvb_sb = sb("bvb_sb", [128, 1024])
        bo_sb = sb("bo_sb", [128, 16])
        o128b_sb = sb("o128b_sb", [128, 1], BF16)
        o1x_sb = sb("o1x_sb", [1, 128])
        iotam_sb = sb("iotam_sb", [128, 8]); iota8_sb = sb("iota8_sb", [128, 8])
        ut_sb = sb("ut_sb", [128, 128])
        stri16_sb = sb("stri16_sb", [16, 16]); ones16_sb = sb("ones16_sb", [16, 1])
        e127_sb = sb("e127_sb", [128, 1])
        idnf_sb = sb("idnf_sb", [128, 128])
        idnb_sb = sb("idnb_sb", [128, 128], BF16)
        wg_sb = sb("wg_sb", [128, 16 * 8], BF16)
        scidx_sb = sb("scidx_sb", [128, 64], I32)
        goidx_sb = sb("goidx_sb", [128, 64], I32)
        plidx_sb = sb("plidx_sb", [2, 1], I32)
        flidx_sb = sb("flidx_sb", [2, 1], I32)
        flag_sb = sb("flag_sb", [2, 64], I32)
        pfl_sb = sb("pfl_sb", [2, 64], I32)
        meidx_sb = sb("meidx_sb", [128, 16])
        eps_sb = sb("eps_sb", [128, 1])
        scl_sb = sb("scl_sb", [128, 1])
        mgate_sb = sb("mgate_sb", [128, 16])
        mlpos_sb = sb("mlpos_sb", [128, 16])

        block = st.enter_context(nc.Block())

        t_bq = em.dma("sync", bq_sb[:], bq)
        t_bk = em.dma("sync", bk_sb[:], bk)
        t_bo = em.dma("sync", bo_sb[:], bocol)
        t_o128 = em.dma("sync", o128b_sb[:], ones128b)
        t_o1x = em.dma("sync", o1x_sb[:], ones1x128)
        t_iom = em.dma("sync", iotam_sb[:], iotam)
        t_io8 = em.dma("sync", iota8_sb[:], iota8b)
        t_ut = em.dma("sync", ut_sb[:], ut128)
        t_s16 = em.dma("sync", stri16_sb[:], stri16)
        t_o16 = em.dma("sync", ones16_sb[:], ones16)
        t_e127 = em.dma("sync", e127_sb[:], e127)
        t_idf = em.dma("sync", idnf_sb[:], idnf)
        t_idb = em.dma("sync", idnb_sb[:], idnb)
        t_wg = em.dma("gpsimd", wg_sb[:].rearrange("p (et j) -> p et j", et=16),
                      wg.rearrange("(et p) j -> p et j", p=128), sem="wq")
        em.dma("sync", scidx_sb[:], scidxd)
        em.dma("sync", goidx_sb[:], goidxd)
        em.dma("sync", plidx_sb[:], plidxd)
        em.dma("sync", flidx_sb[:], flidxd)
        tokEPS = em.dma("sync", eps_sb[:], epsin)
        tokSCL = em.dma("sync", scl_sb[:], sclin)
        consts = [em.tok("sq"), em.tok("wq")]

        red = sb("red", [128, 1]); red2 = sb("red2", [128, 1])
        mmean = sb("mmean", [128, 1]); rstd = sb("rstd", [128, 1])
        sA = ExitStack()
        xch = [sA.enter_context(nc.sbuf_tensor(f"xch{i}", [128, E], F32))
               for i in range(2)]
        lnc = [sA.enter_context(nc.sbuf_tensor(f"lnc{i}", [128, E], BF16))
               for i in range(2)]
        sqv = sA.enter_context(nc.sbuf_tensor("sqv", [128, E], F32))

        def emit_ln(src_ap, dst_ap, waits):
            em.v("reduce_sum", red[:], src_ap, axis=AX.X, waits=waits)
            tokQ = em.s("activation", sqv[:], src_ap, AF.Square, waits=waits)
            em.v("reduce_sum", red2[:], sqv[:], axis=AX.X, waits=[tokQ])
            em.v("tensor_scalar", mmean[:], red[:], 1.0 / E, None, ALU.mult)
            em.v("tensor_scalar", red2[:], red2[:], 1.0 / E, None, ALU.mult)
            em.v("tensor_tensor", red[:], mmean[:], mmean[:], ALU.mult)
            tokV = em.v("tensor_tensor", red2[:], red2[:], red[:], ALU.subtract)
            tokSD = em.s("activation", rstd[:], red2[:], AF.Sqrt,
                         bias=eps_sb[:, 0:1], waits=[tokV, tokEPS])
            tokRI = em.v("reciprocal", rstd[:], rstd[:], waits=[tokSD])
            return em.v("tensor_scalar", dst_ap, src_ap, mmean[:], rstd[:],
                        ALU.subtract, ALU.mult, waits=[tokRI])

        lnw = {}
        for tc in range(32):
            buf = xch[tc % 2]
            t_x = em.dma("sync", buf[:], xb[tc * 128:(tc + 1) * 128, :],
                         waits=[lnw.get(("n", tc % 2), ("dve", 0))])
            tokN = emit_ln(buf[:], lnc[tc % 2][:],
                           [t_x, lnw.get(("s", tc % 2), ("aq", 0))])
            t_st = em.dma("scalar", lnN[tc * 128:(tc + 1) * 128, :],
                          lnc[tc % 2][:], waits=[tokN])
            lnw[("n", tc % 2)] = tokN
            lnw[("s", tc % 2)] = t_st
        lnN_ready = [em.tok("aq")]
        sA.close()
        em.phase()

        sB = ExitStack()
        cos_sb = sB.enter_context(nc.sbuf_tensor("cos_sb", [128, T], F32))
        sin_sb = sB.enter_context(nc.sbuf_tensor("sin_sb", [128, T], F32))
        wsb = sB.enter_context(nc.sbuf_tensor("wsb", [128, 16 * 1024], BF16))
        lnsb = [sB.enter_context(nc.sbuf_tensor(f"lnsb{i}", [128, 16 * 512], BF16))
                for i in range(2)]
        ev_f = sB.enter_context(nc.sbuf_tensor("ev_f", [128, 512], F32))
        rot_f = sB.enter_context(nc.sbuf_tensor("rot_f", [128, 512], F32))
        c1_f = sB.enter_context(nc.sbuf_tensor("c1_f", [128, 512], F32))
        c2_f = sB.enter_context(nc.sbuf_tensor("c2_f", [128, 512], F32))
        stg = [sB.enter_context(nc.sbuf_tensor(f"stg{i}", [128, 512], BF16))
               for i in range(2)]
        vstg = [sB.enter_context(nc.sbuf_tensor(f"vstg{i}", [128, 1024], BF16))
                for i in range(2)]
        psB = ExitStack()
        p_pr = psB.enter_context(nc.psum_tensor("p_pr", [128, 4096], F32))

        def qk_pass(wT_dram, b_sb, b_tok, out_dram, tag):
            t_w = em.dma("gpsimd", wsb[:].rearrange("p (et f) -> p et f", et=16),
                         wT_dram.rearrange("(et p) f -> p et f", p=128),
                         waits=em.phase_toks(), sem="wq")
            last = {("ln", 0): em.tok("pe"), ("ln", 1): em.tok("pe"),
                    ("st", 0): em.tok("aq"), ("st", 1): em.tok("aq"),
                    ("ev", 0): em.tok("dve"), ("ev", 1): em.tok("dve"),
                    ("evs", 0): em.tok("aq"), ("evs", 1): em.tok("aq"),
                    "evf": em.tok("dve"), "evfp": em.tok("pool"),
                    "rot": em.tok("pool"), "c1": em.tok("dve"),
                    "c2": em.tok("dve")}
            for tc in range(8):
                lbuf = lnsb[tc % 2]
                tl = None
                for et in range(16):
                    tl = em.dmat("sync", lbuf[:, et * 512:(et + 1) * 512],
                                 lnN[tc * 512:(tc + 1) * 512,
                                     et * 128:(et + 1) * 128],
                                 waits=(lnN_ready +
                                        [last.get(("ln", tc % 2), ("pe", 0))])
                                 if et == 0 else ())
                for jg in range(2):
                    for et in range(16):
                        for j in range(jg * 4, jg * 4 + 4):
                            w = ()
                            if et == 0 and j == jg * 4:
                                w = [t_w, tl, last.get(("ev", jg), ("dve", 0)),
                                     last.get(("evs", jg), ("aq", 0))]
                            em.t("matmul", p_pr[:, j * 512:(j + 1) * 512],
                                 wsb[:, et * 1024 + j * 128:et * 1024 + (j + 1) * 128],
                                 lbuf[:, et * 512:(et + 1) * 512],
                                 start=(et == 0), stop=(et == 15),
                                 last=(et == 15 and j == jg * 4 + 3), waits=w)
                    mm_tok = em.tok("pe")
                    for j in range(jg * 4, jg * 4 + 4):
                        pr = p_pr[:, j * 512:(j + 1) * 512]
                        tokB = em.v("tensor_scalar", ev_f[:], pr,
                                    b_sb[:, j:j + 1], None, ALU.add,
                                    waits=[mm_tok, b_tok,
                                           last.get("evfp", ("pool", 0))])
                        tokR1 = em.v("tensor_scalar", rot_f[0:64, :],
                                     ev_f[64:128, :], -1.0, None, ALU.mult,
                                     waits=[tokB, last.get("rot", ("pool", 0))])
                        tokR2 = em.v("tensor_copy", rot_f[64:128, :],
                                     ev_f[0:64, :])
                        tokC1 = em.g("tensor_tensor", c1_f[:], ev_f[:],
                                     cos_sb[:, tc * 512:(tc + 1) * 512],
                                     ALU.mult, waits=[tokB, t_cos,
                                                      last.get("c1", ("dve", 0))])
                        tokC2 = em.g("tensor_tensor", c2_f[:], rot_f[:],
                                     sin_sb[:, tc * 512:(tc + 1) * 512],
                                     ALU.mult, waits=[tokR2, t_sin,
                                                      last.get("c2", ("dve", 0))])
                        tokS = em.v("tensor_tensor", stg[j % 2][:], c1_f[:],
                                    c2_f[:], ALU.add,
                                    waits=[tokC1, tokC2,
                                           last.get(("st", j % 2), ("aq", 0))])
                        t_out = em.dma("scalar",
                                       out_dram[j * 128:(j + 1) * 128,
                                                tc * 512:(tc + 1) * 512],
                                       stg[j % 2][:], waits=[tokS])
                        last["evf"] = tokC1
                        last["evfp"] = tokC1
                        last["rot"] = tokC2
                        last["c1"] = tokS
                        last["c2"] = tokS
                        last[("st", j % 2)] = t_out
                        last[("ev", jg)] = tokS
                        last[("evs", jg)] = t_out
                last[("ln", tc % 2)] = em.tok("pe")

        t_cos = em.dma("sync", cos_sb[:], cosd, waits=em.phase_toks())
        t_sin = em.dma("sync", sin_sb[:], sind)
        qk_pass(wqT, bq_sb, t_bq, qTd, "q")
        qk_pass(wkT, bk_sb, t_bk, kTd, "k")

        t_wv = em.dma("gpsimd", wsb[:].rearrange("p (et f) -> p et f", et=16),
                      wvT.rearrange("(et p) f -> p et f", p=128),
                      waits=em.phase_toks(), sem="wq")
        t_bv = em.dma("sync", bvb_sb[0:1, :], bvrow)
        tokBV = em.t("matmul", p_pr[:, 0:512], o1x_sb[:], bvb_sb[0:1, 0:512],
                     start=True, stop=True,
                     waits=[t_bv, t_o1x, em.tok("act"), em.tok("dve")])
        tokBV = em.t("matmul", p_pr[:, 512:1024], o1x_sb[:], bvb_sb[0:1, 512:1024],
                     start=True, stop=True)
        tokBVc = em.v("tensor_copy", bvb_sb[:], p_pr[:, 0:1024], waits=[tokBV])
        last = {("ln", 0): em.tok("pe"), ("ln", 1): em.tok("pe"),
                ("st", 0): em.tok("aq"), ("st", 1): em.tok("aq"),
                ("ev", 0): em.tok("dve"), ("ev", 1): em.tok("dve")}
        for tc in range(8):
            lbuf = lnsb[tc % 2]
            tl = None
            for et in range(16):
                tl = em.dmat("sync", lbuf[:, et * 512:(et + 1) * 512],
                             lnN[tc * 512:(tc + 1) * 512, et * 128:(et + 1) * 128],
                             waits=[last.get(("ln", tc % 2), ("pe", 0))]
                             if et == 0 else ())
            for tt in range(4):
                pv = p_pr[:, 1024 + (tt % 2) * 1024:1024 + (tt % 2) * 1024 + 1024]
                for et in range(16):
                    for nk in range(2):
                        w = ()
                        if et == 0 and nk == 0:
                            w = [t_wv, tl, tokBVc,
                                 last.get(("ev", tt % 2), ("dve", 0))]
                        em.t("matmul", pv[:, nk * 512:(nk + 1) * 512],
                             lbuf[:, et * 512 + tt * 128:et * 512 + (tt + 1) * 128],
                             wsb[:, et * 1024 + nk * 512:et * 1024 + (nk + 1) * 512],
                             start=(et == 0), stop=(et == 15),
                             last=(et == 15 and nk == 1), waits=w)
                mm_tok = em.tok("pe")
                tokV = em.v("tensor_tensor", vstg[tt % 2][:], pv[:], bvb_sb[:],
                            ALU.add, waits=[mm_tok,
                                            last.get(("st", tt % 2), ("aq", 0))])
                t_outv = em.dma("scalar",
                                vNd[tc * 512 + tt * 128:tc * 512 + (tt + 1) * 128, :],
                                vstg[tt % 2][:], waits=[tokV])
                last[("ev", tt % 2)] = tokV
                last[("st", tt % 2)] = t_outv
            last[("ln", tc % 2)] = em.tok("pe")
        qkv_done = [em.tok("aq"), em.tok("pe")]
        psB.close()
        sB.close()
        em.phase()

        sC = ExitStack()
        kh_sb = sC.enter_context(nc.sbuf_tensor("kh_sb", [128, T], BF16))
        dmask_sb = sC.enter_context(nc.sbuf_tensor("dmask_sb", [128, 2048], F32))
        qh_sb = sC.enter_context(nc.sbuf_tensor("qh_sb", [128, T], BF16))
        vh_sb = sC.enter_context(nc.sbuf_tensor("vh_sb", [128, T], BF16))
        exps = [sC.enter_context(nc.sbuf_tensor(f"exps{i}", [128, 1024], BF16))
                for i in range(2)]
        rec_sb = sC.enter_context(nc.sbuf_tensor("rec_sb", [1, 512], F32))
        bcf_sb = sC.enter_context(nc.sbuf_tensor("bcf_sb", [128, 512], F32))
        ynf = [sC.enter_context(nc.sbuf_tensor(f"ynf{i}", [128, 512], F32))
               for i in range(2)]
        psC = ExitStack()
        psS = [psC.enter_context(nc.psum_tensor(f"psS{i}", [128, 1024], F32))
               for i in range(2)]
        psY = psC.enter_context(nc.psum_tensor("psY", [128, 512], F32))
        psD = psC.enter_context(nc.psum_tensor("psD", [1, 512], F32))
        psBC = psC.enter_context(nc.psum_tensor("psBC", [128, 512], F32))

        last = {}
        t_dm = em.dma("sync", dmask_sb[:], dmaskd, waits=em.phase_toks())
        for hj in range(8):
            t_k = em.dma("sync", kh_sb[:], kTd[hj * 128:(hj + 1) * 128, :],
                         waits=(em.phase_toks() if hj == 0 else []) +
                         [last.get("khf", ("pe", 0))])
            t_q = em.dma("sync", qh_sb[:], qTd[hj * 128:(hj + 1) * 128, :])
            t_v = em.dma("sync",
                         vh_sb[:].rearrange("p (tt d) -> p tt d", tt=32),
                         vNd[:, hj * 128:(hj + 1) * 128].rearrange(
                             "(tt p) d -> p tt d", p=128))
            for qc in range(8):
                npair = 2 * (qc + 1)
                for pb in range(npair):
                    ps = psS[pb % 2]
                    exb = exps[pb % 2]
                    w0 = [t_k, t_q, last.get(("exp", pb % 2), ("act", 0))]
                    for half in range(2):
                        kb = pb * 2 + half
                        em.t("matmul", ps[:, half * 512:(half + 1) * 512],
                             kh_sb[:, kb * 128:(kb + 1) * 128],
                             qh_sb[:, qc * 512:(qc + 1) * 512],
                             start=True, stop=True,
                             last=(half == 1), waits=w0 if half == 0 else ())
                    mm_tok = em.tok("pe")
                    mw = [mm_tok]
                    if pb >= npair - 2:
                        for half in range(2):
                            kb = pb * 2 + half
                            v = kb - qc * 4
                            if 0 <= v <= 3:
                                tokM = em.v("tensor_tensor",
                                            ps[:, half * 512:(half + 1) * 512],
                                            ps[:, half * 512:(half + 1) * 512],
                                            dmask_sb[:, v * 512:(v + 1) * 512],
                                            ALU.add, waits=[mm_tok, t_dm])
                                mw = [tokM]
                    tokE = em.s("activation", exb[:], ps[:], AF.Exp,
                                scale=scl_sb[:, 0:1],
                                waits=mw + [tokSCL,
                                            last.get(("eu", pb % 2), ("pe", 0))])
                    last[("exp", pb % 2)] = tokE
                    for half in range(2):
                        is_last = (pb == npair - 1 and half == 1)
                        wD = [tokE]
                        wY = [tokE]
                        if pb == 0 and half == 0:
                            wD += [t_o128, last.get("recD", ("dve", 0))]
                            wY += [t_v, last.get("recY", ("dve", 0))]
                        em.t("matmul", psD[:],
                             o128b_sb[:], exb[:, half * 512:(half + 1) * 512],
                             start=(pb == 0 and half == 0), stop=is_last,
                             last=False, waits=wD)
                        kb = pb * 2 + half
                        em.t("matmul", psY[:],
                             vh_sb[:, kb * 128:(kb + 1) * 128],
                             exb[:, half * 512:(half + 1) * 512],
                             start=(pb == 0 and half == 0), stop=is_last,
                             last=is_last, waits=wY)
                    last[("eu", pb % 2)] = em.tok("pe")
                pe_tok = em.tok("pe")
                tokR = em.v("reciprocal", rec_sb[:], psD[:], waits=[pe_tok])
                last["recD"] = tokR
                tokBCm = em.t("matmul", psBC[:], o1x_sb[:], rec_sb[:],
                              start=True, stop=True, waits=[tokR, t_o1x])
                tokBC = em.v("tensor_copy", bcf_sb[:], psBC[:], waits=[tokBCm])
                tokYN = em.v("tensor_tensor", ynf[qc % 2][:], psY[:], bcf_sb[:],
                             ALU.mult,
                             waits=[tokBC,
                                    last.get(("yst", qc % 2), ("gq", 0))])
                last["recY"] = tokYN
                t_y = em.op("gpsimd", "indirect_dma_start",
                            shwr, IndirectOffsetOnAxis(
                                scidx_sb[:, hj * 8 + qc:hj * 8 + qc + 1], 0),
                            ynf[qc % 2][:].bitcast(I32), None,
                            waits=[tokYN, consts[0]], sem="gq", inc=16,
                            bounds_check=2 * 8192 - 1, oob_is_err=False)
                last[("yst", qc % 2)] = t_y
            last["khf"] = em.tok("pe")
        attn_done = [em.tok("gq")]
        psC.close()
        sC.close()
        em.phase()

        tokF = em.v("memset", flag_sb[:], MAGIC)
        em.op("gpsimd", "indirect_dma_start",
              shflag, IndirectOffsetOnAxis(flidx_sb[:], 0), flag_sb[:], None,
              waits=attn_done + [tokF, consts[0]], sem="gq", inc=16,
              bounds_check=3, oob_is_err=False)
        if not dev_stub:
            sems = em.sems

            def poll_play(eng):
                cnt = eng.alloc_register("pollcnt")
                nd = eng.alloc_register("pollnd")
                eng.reg_mov(cnt, 0)
                eng.reg_mov(nd, 1)
                with eng.While(nd):
                    eng.reg_add(cnt, cnt, 16)
                    eng.indirect_dma_start(
                        pfl_sb[:], None, shflag,
                        IndirectOffsetOnAxis(plidx_sb[:], 0),
                        bounds_check=3, oob_is_err=False).then_inc(sems["pq"], 16)
                    eng.wait_ge(sems["pq"], cnt)
                    v = eng.value_load(pfl_sb[0:1, 0:1])
                    eng.reg_alu(nd, v, MAGIC, ALU.not_equal)
            em.raw("gpsimd", poll_play)

        sE = ExitStack()
        wo_sb = sE.enter_context(nc.sbuf_tensor("wo_sb", [128, 16 * 2048], BF16))
        ybnc = [sE.enter_context(nc.sbuf_tensor(f"ybnc{i}", [128, 512], I32))
                for i in range(2)]
        ypb = [sE.enter_context(nc.sbuf_tensor(f"ypb{i}", [128, 16 * 512], BF16))
               for i in range(2)]
        cmb_sb = sE.enter_context(nc.sbuf_tensor("cmb_sb", [128, 512], F32))
        attn_n = [sE.enter_context(nc.sbuf_tensor(f"attn_n{i}", [128, E], F32))
                  for i in range(4)]
        xoch = sE.enter_context(nc.sbuf_tensor("xoch", [128, E], F32))
        xmch = sE.enter_context(nc.sbuf_tensor("xmch", [128, E], F32))
        sqv2 = sE.enter_context(nc.sbuf_tensor("sqv2", [128, E], F32))
        h2c = sE.enter_context(nc.sbuf_tensor("h2c", [128, E], F32))
        h2cb = sE.enter_context(nc.sbuf_tensor("h2cb", [128, E], BF16))
        trsb = [sE.enter_context(nc.sbuf_tensor(f"trsb{i}", [128, 128], BF16))
                for i in range(2)]
        prob_sb = sE.enter_context(nc.sbuf_tensor("prob_sb", [128, 8], F32))
        nmax_sb = sE.enter_context(nc.sbuf_tensor("nmax_sb", [128, 1], F32))
        dsum_sb = sE.enter_context(nc.sbuf_tensor("dsum_sb", [128, 1], F32))
        eq8_sb = sE.enter_context(nc.sbuf_tensor("eq8_sb", [128, 8], F32))
        t8_sb = sE.enter_context(nc.sbuf_tensor("t8_sb", [128, 8], F32))
        psE = ExitStack()
        ptr = [psE.enter_context(nc.psum_tensor(f"ptr{i}", [128, 128], F32))
               for i in range(2)]
        ptrb = [psE.enter_context(nc.psum_tensor(f"ptrb{i}", [128, 128], BF16))
                for i in range(2)]
        psG = psE.enter_context(nc.psum_tensor("psG", [128, 8], F32))
        psO = [psE.enter_context(nc.psum_tensor(f"psO{i}", [128, 512], F32))
               for i in range(2)]

        t_wo = em.dma("gpsimd", wo_sb[:].rearrange("p (et f) -> p et f", et=16),
                      woT.rearrange("(et p) f -> p et f", p=128),
                      waits=em.phase_toks(), sem="wq")
        last = {}
        for i in range(4):
            for eg in range(16):
                t_g = em.gather(ybnc[eg % 2][:], shrd,
                                goidx_sb[:, i * 16 + eg:i * 16 + eg + 1],
                                2 * 8192 - 1,
                                waits=[last.get(("ybw", eg % 2), ("dve", 0)),
                                       consts[0]])
                tokYC = em.v("tensor_copy",
                             ypb[i % 2][:, eg * 512:(eg + 1) * 512],
                             ybnc[eg % 2][:].bitcast(F32),
                             waits=[t_g, last.get(("ypu", i % 2), ("pe", 0))])
                last[("ybw", eg % 2)] = tokYC
            yp_ready = em.tok("dve")
            for ft in range(16):
                po = psO[ft % 2]
                for eg in range(16):
                    w = ()
                    if eg == 0:
                        w = [t_wo, yp_ready,
                             last.get(("ev", ft % 2), ("dve", 0))]
                    em.t("matmul", po[:],
                         wo_sb[:, eg * 2048 + ft * 128:eg * 2048 + (ft + 1) * 128],
                         ypb[i % 2][:, eg * 512:(eg + 1) * 512],
                         start=(eg == 0), stop=(eg == 15),
                         last=(eg == 15), waits=w)
                mm_tok = em.tok("pe")
                tokA2 = em.v("tensor_scalar", cmb_sb[:], po[:],
                             bo_sb[:, ft:ft + 1], None, ALU.add,
                             waits=[mm_tok, t_bo, last.get("cmb", ("pe", 0))])
                last[("ev", ft % 2)] = tokA2
                for q4 in range(4):
                    tokT = em.t("transpose", ptr[q4 % 2][:],
                                cmb_sb[:, q4 * 128:(q4 + 1) * 128], idnf_sb[:],
                                waits=[tokA2, t_idf,
                                       last.get(("trc", q4 % 2), ("dve", 0))])
                    tokTc = em.v("tensor_copy", attn_n[q4][:, ft * 128:(ft + 1) * 128],
                                 ptr[q4 % 2][:], waits=[tokT])
                    last[("trc", q4 % 2)] = tokTc
                last["cmb"] = em.tok("pe")
            last[("ypu", i % 2)] = em.tok("pe")
            attw = [em.tok("dve")]
            for q4 in range(4):
                tc128 = i * 4 + q4
                t_xo = em.dma("sync", xoch[:],
                              xown[tc128 * 128:(tc128 + 1) * 128, :],
                              waits=[last.get("xo", ("dve", 0))])
                tokXM = em.v("tensor_tensor", xmch[:], attn_n[q4][:], xoch[:],
                             ALU.add, waits=attw + [t_xo,
                                                    last.get("xm", ("aq", 0)),
                                                    last.get("xmq", ("act", 0))])
                last["xo"] = tokXM
                t_xm = em.dma("scalar", xmid_o[tc128 * 128:(tc128 + 1) * 128, :],
                              xmch[:], waits=[tokXM])
                last["xm"] = t_xm
                em.v("reduce_sum", red[:], xmch[:], axis=AX.X)
                tokQ = em.s("activation", sqv2[:], xmch[:], AF.Square,
                            waits=[tokXM])
                last["xmq"] = tokQ
                em.v("reduce_sum", red2[:], sqv2[:], axis=AX.X, waits=[tokQ])
                em.v("tensor_scalar", mmean[:], red[:], 1.0 / E, None, ALU.mult)
                em.v("tensor_scalar", red2[:], red2[:], 1.0 / E, None, ALU.mult)
                em.v("tensor_tensor", red[:], mmean[:], mmean[:], ALU.mult)
                tokV2 = em.v("tensor_tensor", red2[:], red2[:], red[:],
                             ALU.subtract)
                tokSD = em.s("activation", rstd[:], red2[:], AF.Sqrt,
                             bias=eps_sb[:, 0:1], waits=[tokV2, tokEPS])
                tokRI = em.v("reciprocal", rstd[:], rstd[:], waits=[tokSD])
                tokH = em.v("tensor_scalar", h2c[:], xmch[:], mmean[:], rstd[:],
                            ALU.subtract, ALU.mult,
                            waits=[tokRI, last.get("h2r", ("pe", 0))])
                t_h2 = em.dma("scalar", h2o_o[tc128 * 128:(tc128 + 1) * 128, :],
                              h2c[:], waits=[tokH])
                t_h2b = em.dma("scalar", h2f[tc128 * 128:(tc128 + 1) * 128, :],
                               h2c[:], waits=[tokH])
                last["xm"] = t_h2b
                tokHB = em.v("tensor_copy", h2cb[:], h2c[:],
                             waits=[tokH, last.get("h2br", ("pe", 0))])
                for et in range(16):
                    tokT = em.t("transpose", ptrb[et % 2][:],
                                h2cb[:, et * 128:(et + 1) * 128],
                                idnb_sb[:],
                                waits=[tokHB, t_idb,
                                       last.get(("tru", et % 2), ("dve", 0))])
                    tokTc = em.v("tensor_copy", trsb[et % 2][:], ptrb[et % 2][:],
                                 waits=[tokT])
                    last[("tru", et % 2)] = tokTc
                    em.t("matmul", psG[:], trsb[et % 2][:],
                         wg_sb[:, et * 8:(et + 1) * 8],
                         start=(et == 0), stop=(et == 15), last=(et == 15),
                         waits=[tokTc, t_wg] +
                         ([last.get("psg", ("dve", 0)),
                           last.get("psga", ("act", 0))] if et == 0 else []))
                last["h2r"] = em.tok("pe")
                last["h2br"] = em.tok("pe")
                pg_tok = em.tok("pe")
                tokNM = em.v("reduce_max", nmax_sb[:], psG[:], axis=AX.X,
                             negate=True, waits=[pg_tok])
                tokPR = em.s("activation", prob_sb[:], psG[:], AF.Exp,
                             bias=nmax_sb[:], accum_out=dsum_sb[:],
                             waits=[pg_tok, tokNM])
                tokGT = em.v("reciprocal", mgate_sb[:, tc128:tc128 + 1],
                             dsum_sb[:], waits=[tokPR])
                tokEQ = em.v("tensor_scalar", eq8_sb[:], psG[:], nmax_sb[:], 0.0,
                             ALU.add, ALU.is_ge, waits=[tokNM, pg_tok])
                last["psg"] = tokEQ
                last["psga"] = tokPR
                tokT8 = em.v("tensor_tensor", t8_sb[:], eq8_sb[:], iotam_sb[:],
                             ALU.mult, waits=[t_iom])
                tokT9 = em.v("tensor_scalar", t8_sb[:], t8_sb[:], 1000.0, None,
                             ALU.add)
                em.v("tensor_reduce", meidx_sb[:, tc128:tc128 + 1], t8_sb[:],
                     axis=AX.X, op=ALU.min)
        gate_done = [em.tok("dve"), em.tok("act")]
        psE.close()
        sE.close()
        em.phase()

        sF = ExitStack()
        OH2 = sF.enter_context(nc.sbuf_tensor("OH2", [128, 128], F32))
        cumS = sF.enter_context(nc.sbuf_tensor("cumS", [128, 128], F32))
        totrow = sF.enter_context(nc.sbuf_tensor("totrow", [1, 128], F32))
        totP = sF.enter_context(nc.sbuf_tensor("totP", [16, 8], F32))
        gpre = sF.enter_context(nc.sbuf_tensor("gpre", [16, 8], F32))
        cnt1 = sF.enter_context(nc.sbuf_tensor("cnt1", [1, 8], F32))
        gprow = sF.enter_context(nc.sbuf_tensor("gprow", [1, 128], F32))
        gpb = sF.enter_context(nc.sbuf_tensor("gpb", [128, 128], F32))
        posP = sF.enter_context(nc.sbuf_tensor("posP", [128, 128], F32))
        selp = sF.enter_context(nc.sbuf_tensor("selp", [128, 128], F32))
        scf = sF.enter_context(nc.sbuf_tensor("scf", [128, 16], F32))
        scov = sF.enter_context(nc.sbuf_tensor("scov", [128, 16], F32))
        sci = sF.enter_context(nc.sbuf_tensor("sci", [128, 16], I32))
        psF = ExitStack()
        pC = psF.enter_context(nc.psum_tensor("pC", [128, 128], F32))
        pT = psF.enter_context(nc.psum_tensor("pT", [1, 128], F32))
        pG16 = psF.enter_context(nc.psum_tensor("pG16", [16, 8], F32))
        pC8 = psF.enter_context(nc.psum_tensor("pC8", [1, 8], F32))
        pB2 = psF.enter_context(nc.psum_tensor("pB2", [128, 128], F32))

        for g in range(16):
            em.v("tensor_scalar", OH2[:, g * 8:(g + 1) * 8], iota8_sb[:],
                 meidx_sb[:, g:g + 1], None, ALU.is_equal,
                 waits=(em.phase_toks() if g == 0 else []) + [t_io8])
        tokOH = em.tok("dve")
        tokCU = em.t("matmul", pC[:], ut_sb[:], OH2[:], start=True, stop=True,
                     waits=[tokOH, t_ut])
        tokCS = em.v("tensor_copy", cumS[:], pC[:], waits=[tokCU])
        tokTT = em.t("matmul", pT[:], e127_sb[:], cumS[:], start=True, stop=True,
                     waits=[tokCS, t_e127])
        tokTR = em.v("tensor_copy", totrow[:], pT[:], waits=[tokTT])
        t_b1 = em.dma("scalar", bnc, totrow[:], waits=[tokTR])
        t_b2 = em.dma("sync", totP[:], bnc.rearrange("o (g e) -> (o g) e", g=16),
                      waits=[t_b1])
        tokG16 = em.t("matmul", pG16[:], stri16_sb[:], totP[:], start=True,
                      stop=True, waits=[t_b2, t_s16])
        tokC8 = em.t("matmul", pC8[:], ones16_sb[:], totP[:], start=True,
                     stop=True, waits=[t_o16])
        tokGP = em.v("tensor_copy", gpre[:], pG16[:], waits=[tokG16])
        tokCN = em.v("tensor_copy", cnt1[:], pC8[:], waits=[tokC8])
        t_cn = em.dma("scalar", cnt_o, cnt1[:], waits=[tokCN])
        t_b3 = em.dma("scalar", bnc2, gpre[:], waits=[tokGP])
        t_b4 = em.dma("sync", gprow[:],
                      bnc2.rearrange("(o g) e -> o (g e)", o=1), waits=[t_b3])
        tokB2 = em.t("matmul", pB2[:], o1x_sb[:], gprow[:], start=True, stop=True,
                     waits=[t_b4, t_o1x])
        tokPP = em.v("tensor_tensor", posP[:], cumS[:], pB2[:], ALU.add,
                     waits=[tokB2])
        tokSel = em.v("tensor_tensor", selp[:], OH2[:], posP[:], ALU.mult)
        tokRed = em.v("tensor_reduce",
                      mlpos_sb[:].rearrange("p (g e) -> p g e", g=1),
                      selp[:].rearrange("p (g e) -> p g e", g=16),
                      axis=AX.X, op=ALU.add, waits=[tokSel])
        tokLP = em.v("tensor_scalar", mlpos_sb[:], mlpos_sb[:], -1.0, None,
                     ALU.add)
        em.v("tensor_scalar", scf[:], meidx_sb[:], float(LCAP), None, ALU.mult,
             waits=[tokLP])
        em.v("tensor_tensor", scf[:], scf[:], mlpos_sb[:], ALU.add)
        em.v("tensor_scalar", scov[:], mlpos_sb[:], float(LCAP) - 0.5, None,
             ALU.is_ge)
        em.v("tensor_scalar", scov[:], scov[:], 65536.0, None, ALU.mult)
        em.v("tensor_tensor", scf[:], scf[:], scov[:], ALU.add)
        tokSCI = em.v("tensor_copy", sci[:], scf[:])
        zb = sF.enter_context(nc.sbuf_tensor("zb_l1", [128, E], F32))
        h2t = [sF.enter_context(nc.sbuf_tensor(f"h2t{i}", [128, E], F32))
               for i in range(2)]
        tokZ = em.dma("sync", zb[:], zrows)
        for zi in range(NE * LCAP // 128):
            em.dma("scalar", h2s_o[zi * 128:(zi + 1) * 128, :], zb[:],
                   waits=[tokZ])
        zdone = em.tok("aq")
        scw = {}
        for g in range(16):
            t_l = em.dma("sync", h2t[g % 2][:], h2f[g * 128:(g + 1) * 128, :],
                         waits=[scw.get(g % 2, ("gq", 0))])
            em.op("gpsimd", "indirect_dma_start",
                  h2s_o, IndirectOffsetOnAxis(sci[:, g:g + 1], 0),
                  h2t[g % 2][:], None,
                  waits=[tokSCI, zdone, t_l], sem="gq", inc=16,
                  bounds_check=NE * LCAP - 1, oob_is_err=False)
            scw[g % 2] = em.tok("gq")
        em.dma("scalar", meidx_o, meidx_sb[:], waits=[em.tok("dve")])
        em.dma("scalar", mgate_o, mgate_sb[:])
        em.dma("scalar", mlpos_o, mlpos_sb[:])
        psF.close()
        sF.close()

        em.finish("sync")
        em.playback(block)
    return nc



def l1_in_maps(inp):
    x = np.asarray(inp["L0_attn_x"]).reshape(B, T, E)
    WqT = np.ascontiguousarray(np.asarray(inp["L0_attn_Wq"]).T)
    WkT = np.ascontiguousarray(np.asarray(inp["L0_attn_Wk"]).T)
    WvT = np.ascontiguousarray(np.asarray(inp["L0_attn_Wv"]).T)
    WoT = np.ascontiguousarray(np.asarray(inp["L0_attn_Wo"]).T)
    bq = np.asarray(inp["L0_attn_bq"]); bk = np.asarray(inp["L0_attn_bk"])
    bv = np.asarray(inp["L0_attn_bv"]); bo = np.asarray(inp["L0_attn_bo"])
    wg = np.ascontiguousarray(np.asarray(inp["L0_ffn_Wg"]))

    j = np.arange(64, dtype=np.float64)
    thetas = 10000.0 ** (-2.0 * j / 128.0)
    m = np.arange(1, T + 1, dtype=np.float64)
    ang = m[None, :] * thetas[:, None]
    cosd = np.concatenate([np.cos(ang), np.cos(ang)], 0).astype(np.float32)
    sind = np.concatenate([np.sin(ang), np.sin(ang)], 0).astype(np.float32)

    dmask = np.zeros((128, 2048), np.float32)
    ki = np.arange(128)[:, None]
    qi = np.arange(512)[None, :]
    for v in range(4):
        valid = (qi >= ki + 128 * v)
        dmask[:, v * 512:(v + 1) * 512] = np.where(valid, 0.0, -30000.0)

    ut = np.zeros((128, 128), np.float32)
    for a in range(128):
        ut[a, a:] = 1.0
    stri16 = np.zeros((16, 16), np.float32)
    for a in range(16):
        stri16[a, a + 1:] = 1.0
    e127 = np.zeros((128, 1), np.float32); e127[127, 0] = 1.0
    iota8 = np.tile(np.arange(8, dtype=np.float32)[None, :], (128, 1))

    maps = []
    for c in range(NC):
        b, hh = c // 2, c % 2
        par = hh
        p = np.arange(128)
        scidx = np.zeros((128, 64), np.int32)
        for hj in range(8):
            for qc in range(8):
                scidx[:, hj * 8 + qc] = par * 8192 + (hj * 128 + p) * 8 + qc
        goidx = np.zeros((128, 64), np.int32)
        for i in range(4):
            for eg in range(16):
                goidx[:, i * 16 + eg] = ((eg // 8) * 8192
                                         + ((eg % 8) * 128 + p) * 8
                                         + hh * 4 + i)
        maps.append({
            "xb": np.ascontiguousarray(x[b]),
            "xown": np.ascontiguousarray(x[b, hh * NT:(hh + 1) * NT]),
            "wqT": np.ascontiguousarray(WqT[:, hh * 1024:(hh + 1) * 1024]),
            "wkT": np.ascontiguousarray(WkT[:, hh * 1024:(hh + 1) * 1024]),
            "wvT": np.ascontiguousarray(WvT[:, hh * 1024:(hh + 1) * 1024]),
            "woT": WoT,
            "bq": np.ascontiguousarray(
                bq[hh * 1024:(hh + 1) * 1024].reshape(8, 128).T),
            "bk": np.ascontiguousarray(
                bk[hh * 1024:(hh + 1) * 1024].reshape(8, 128).T),
            "bvrow": np.ascontiguousarray(bv[hh * 1024:(hh + 1) * 1024][None, :]),
            "bocol": np.ascontiguousarray(bo.reshape(16, 128).T),
            "wg": wg,
            "cosd": cosd, "sind": sind,
            "dmaskd": dmask,
            "ones128b": np.ones((128, 1), BF),
            "ones1x128": np.ones((1, 128), np.float32),
            "iotam": iota8 - 1000.0,
            "iota8b": iota8,
            "ut128": ut,
            "stri16": stri16,
            "ones16": np.ones((16, 1), np.float32),
            "e127": e127,
            "idnf": np.eye(128, dtype=np.float32),
            "idnb": np.eye(128, dtype=BF),
            "epsin": np.full((128, 1), 1.0e-5, np.float32),
            "sclin": np.full((128, 1), INVSQ, np.float32),
            "zrows": np.zeros((128, E), np.float32),
            "scidxd": scidx, "goidxd": goidx,
            "plidxd": np.array([[(1 - par) * 2], [(1 - par) * 2 + 1]], np.int32),
            "flidxd": np.array([[par * 2], [par * 2 + 1]], np.int32),
        })
    return maps



def kernel(L0_attn_x, L0_attn_Wq, L0_attn_bq, L0_attn_Wk, L0_attn_bk,
           L0_attn_Wv, L0_attn_bv, L0_attn_Wo, L0_attn_bo, L0_ffn_Wg,
           L0_ffn_W1, L0_ffn_W3, L0_ffn_W2, L0_ffn_sw1, L0_ffn_sw3,
           L0_ffn_sw2):
    inp = dict(
        L0_attn_x=np.asarray(L0_attn_x), L0_attn_Wq=np.asarray(L0_attn_Wq),
        L0_attn_bq=np.asarray(L0_attn_bq), L0_attn_Wk=np.asarray(L0_attn_Wk),
        L0_attn_bk=np.asarray(L0_attn_bk), L0_attn_Wv=np.asarray(L0_attn_Wv),
        L0_attn_bv=np.asarray(L0_attn_bv), L0_attn_Wo=np.asarray(L0_attn_Wo),
        L0_attn_bo=np.asarray(L0_attn_bo), L0_ffn_Wg=np.asarray(L0_ffn_Wg),
        L0_ffn_W1=np.asarray(L0_ffn_W1), L0_ffn_W3=np.asarray(L0_ffn_W3),
        L0_ffn_W2=np.asarray(L0_ffn_W2), L0_ffn_sw1=np.asarray(L0_ffn_sw1),
        L0_ffn_sw3=np.asarray(L0_ffn_sw3), L0_ffn_sw2=np.asarray(L0_ffn_sw2),
    )
    cores = list(range(NC))

    maps1 = l1_in_maps(inp)
    nc1 = build_l1()
    r1 = run_bass_kernel_spmd(nc1, maps1, core_ids=cores).results

    counts_all = np.concatenate([np.asarray(r1[c]["cnt"]) for c in cores], axis=0)
    h2sorted_all = [np.asarray(r1[c]["h2s"]) for c in cores]
    maps2 = l2_in_maps(h2sorted_all, counts_all,
                       inp["L0_ffn_W1"], inp["L0_ffn_W3"], inp["L0_ffn_W2"])
    nc2 = build_l2()
    r2 = run_bass_kernel_spmd(nc2, maps2, core_ids=cores).results

    oe_blocks = [np.asarray(r2[e]["oe"]) for e in cores]
    h2own_l = [np.asarray(r1[c]["h2o"]) for c in cores]
    xmid_l = [np.asarray(r1[c]["xmid"]) for c in cores]
    meta_l = [(np.asarray(r1[c]["m_eidx"]), np.asarray(r1[c]["m_gate"]),
               np.asarray(r1[c]["m_lpos"])) for c in cores]
    maps3 = l3_in_maps(oe_blocks, h2own_l, xmid_l, meta_l, counts_all,
                       inp["L0_ffn_sw1"], inp["L0_ffn_sw3"], inp["L0_ffn_sw2"])
    nc3 = build_l3()
    r3 = run_bass_kernel_spmd(nc3, maps3, core_ids=cores).results

    out = np.concatenate([np.asarray(r3[c]["out"]) for c in cores], axis=0)
    return np.ascontiguousarray(out.reshape(B, T, E).astype(np.float32))
